# version 01
# baseline (speedup 1.0000x reference)
import contextlib

import numpy as np

import concourse.bass as bass
import concourse.mybir as mybir
from concourse.bass_utils import run_bass_kernel_spmd

F32 = mybir.dt.float32
BF16 = mybir.dt.bfloat16
FP8 = mybir.dt.float8e4
AX = mybir.AxisListType.X
AF = mybir.ActivationFunctionType
OP = mybir.AluOpType

NCORES = 8
D = 2048
DC = 16
TOK = 1024
HALO = 512
TKV = TOK + HALO
NH = 16
FI = 6144
FC = 48
SCALE = float(1.0 / np.sqrt(128.0))
EPS = 1e-5
ENGS = ("sync", "gpsimd", "vector", "scalar", "tensor")


class Ctr:
    def __init__(self, h):
        self.h = h
        self.n = 0

    def bump(self, k):
        self.n += k
        return self.n


def build_nc(upto=99):
    import os
    ksub = os.environ.get("KSUB", "")
    nc = bass.Bass("TRN2", target_bir_lowering=False,
                   detect_race_conditions=os.environ.get("KRACE", "1") == "1")

    d_xo = nc.dram_tensor("xo", [D, TOK], F32, kind="ExternalInput").ap()
    d_xh = nc.dram_tensor("xh", [D, HALO], F32, kind="ExternalInput").ap()
    d_wqkvT = nc.dram_tensor("wqkvT", [D, 3 * D], F32, kind="ExternalInput").ap()
    d_woutT = nc.dram_tensor("woutT", [D, D], F32, kind="ExternalInput").ap()
    d_gw = nc.dram_tensor("gw", [D, FI], F32, kind="ExternalInput").ap()
    d_uw = nc.dram_tensor("uw", [D, FI], F32, kind="ExternalInput").ap()
    d_dw = nc.dram_tensor("dw", [FI, D], F32, kind="ExternalInput").ap()
    d_gsc = nc.dram_tensor("gscT", [128, FC], F32, kind="ExternalInput").ap()
    d_usc = nc.dram_tensor("uscT", [128, FC], F32, kind="ExternalInput").ap()
    d_dsc = nc.dram_tensor("dscT", [128, DC], F32, kind="ExternalInput").ap()
    d_cT = nc.dram_tensor("cT", [128, DC], F32, kind="ExternalInput").ap()
    d_wmods = nc.dram_tensor("wmods", [D, 6 * D], F32, kind="ExternalInput").ap()
    d_halo = nc.dram_tensor("halo_on", [128, 1], F32, kind="ExternalInput").ap()
    d_out = nc.dram_tensor("outT", [D, TOK], F32, kind="ExternalOutput").ap()
    d_x2 = nc.dram_tensor("x2scr", [D, TOK], F32).ap()
    d_mhi = nc.dram_tensor("modscr_hi", [96, 128], BF16).ap()
    d_mlo = nc.dram_tensor("modscr_lo", [96, 128], BF16).ap()
    d_mrows = nc.dram_tensor("modrows", [96, 128], F32).ap()

    ops = {e: [] for e in ENGS}
    last_wait = {}
    dbg_h1 = [None]
    dbg_x2 = [None]

    class _Done(Exception):
        pass


    def emit(eng, fn):
        ops[eng].append(fn)

    try:
      with contextlib.ExitStack() as top:
        en = top.enter_context
        dS = Ctr(en(nc.semaphore("dS")))
        oS = Ctr(en(nc.semaphore("oS")))
        tS = Ctr(en(nc.semaphore("tS")))
        vS = Ctr(en(nc.semaphore("vS")))
        aS = Ctr(en(nc.semaphore("aS")))
        gS = Ctr(en(nc.semaphore("gS")))
        d0 = Ctr(en(nc.semaphore("d0")))
        wmS = [Ctr(en(nc.semaphore(f"wmS{i}"))) for i in range(2)]
        msS = Ctr(en(nc.semaphore("msS")))
        mcS = Ctr(en(nc.semaphore("mcS")))
        dXh = Ctr(en(nc.semaphore("dXh")))
        dXo = Ctr(en(nc.semaphore("dXo")))
        stgS = [Ctr(en(nc.semaphore(f"stgS{i}"))) for i in range(2)]
        xtS = [Ctr(en(nc.semaphore(f"xtS{i}"))) for i in range(4)]
        spS = Ctr(en(nc.semaphore("spS")))
        x2S = [Ctr(en(nc.semaphore(f"x2S{i}"))) for i in range(2)]
        obS = [Ctr(en(nc.semaphore(f"obS{i}"))) for i in range(2)]

        def W(eng, ctr, val):
            if val is None or val <= 0:
                return
            key = (eng, id(ctr))
            if last_wait.get(key, -1) >= val:
                return
            last_wait[key] = val
            emit(eng, lambda e, h=ctr.h, v=val: e.wait_ge(h, v))

        def dma(dst, src, ctr=dS):
            emit("sync", lambda e, a=dst, b=src, h=ctr.h: e.dma_start(
                out=a, in_=b).then_inc(h, 16))
            return ctr.bump(16)

        def mm(out, lhsT, rhs, start, stop, inc=False):
            if inc:
                emit("tensor", lambda e, o=out, l=lhsT, r=rhs, s=start,
                     p=stop, h=tS.h: e.matmul(o, l, r, start=s, stop=p)
                     .then_inc(h, 1))
                return tS.bump(1)
            emit("tensor", lambda e, o=out, l=lhsT, r=rhs, s=start,
                 p=stop: e.matmul(o, l, r, start=s, stop=p))
            return None

        def ldw(w):
            emit("tensor", lambda e, a=w: e.ldweights(a))

        def tmark():
            emit("tensor", lambda e, h=tS.h: e.nop().then_inc(h, 1))
            return tS.bump(1)

        def vop(fn, inc=True):
            if inc:
                emit("vector", lambda e, f=fn, h=vS.h: f(e).then_inc(h, 1))
                return vS.bump(1)
            emit("vector", fn)
            return None

        def aop(fn, inc=True):
            if inc:
                emit("scalar", lambda e, f=fn, h=aS.h: f(e).then_inc(h, 1))
                return aS.bump(1)
            emit("scalar", fn)
            return None

        def vfence():
            W("vector", vS, vS.n)
            return vop(lambda e: e.engine_nop())

        def afence():
            W("scalar", aS, aS.n)
            return aop(lambda e: e.copy(fsc[0:1, 0:1], consts[0:1, 0:1]))

        def gop(fn, inc=True):
            if inc:
                emit("gpsimd", lambda e, f=fn, h=gS.h: f(e).then_inc(h, 1))
                return gS.bump(1)
            emit("gpsimd", fn)
            return None

        consts = en(nc.sbuf_tensor("consts", [128, 4], F32))
        onescb = en(nc.sbuf_tensor("onescb", [128, 1], BF16))
        ones1b = en(nc.sbuf_tensor("ones1b", [1, 128], BF16))
        ones1f = en(nc.sbuf_tensor("ones1f", [1, 128], F32))
        idb = en(nc.sbuf_tensor("idb", [128, 128], BF16))
        diagsc = en(nc.sbuf_tensor("diagsc", [128, 128], BF16))


        m_le = en(nc.sbuf_tensor("m_le", [128, 128], BF16))
        m_ge = en(nc.sbuf_tensor("m_ge", [128, 128], BF16))
        halocol = en(nc.sbuf_tensor("halocol", [128, 1], F32))
        modcol = en(nc.sbuf_tensor("modcol", [128, 96], F32))
        modp = en(nc.sbuf_tensor("modp", [128, 96], F32))
        silucT = en(nc.sbuf_tensor("silucT", [128, DC], F32))
        cTs = en(nc.sbuf_tensor("cTs", [128, DC], F32))
        gscs = en(nc.sbuf_tensor("gscs", [128, FC], F32))
        uscs = en(nc.sbuf_tensor("uscs", [128, FC], F32))
        dscs = en(nc.sbuf_tensor("dscs", [128, DC], F32))
        dscgb = en(nc.sbuf_tensor("dscgb", [128, DC], F32))
        rows = en(nc.sbuf_tensor("rows", [1, 2048], F32))
        fsc = en(nc.sbuf_tensor("fsc", [1, 4], F32))
        stg = en(nc.sbuf_tensor("stg", [128, 2 * 2048], F32))
        PB = [en(nc.psum_tensor(f"PB{i}", [128, 512], F32)) for i in range(8)]

        cvals = [SCALE, -1.0, -0.5, EPS]
        for j, val in enumerate(cvals):
            nc.const_aps.register(F32, float(val), consts[:, j:j + 1])

        sa_c = lambda dc: modcol[:, 0 + dc:1 + dc]
        sb_c = lambda dc: modcol[:, 48 + dc:49 + dc]
        ca1_c = lambda dc: modp[:, 16 + dc:17 + dc]
        cb1_c = lambda dc: modp[:, 64 + dc:65 + dc]
        ga_c = lambda dc: modp[:, 32 + dc:33 + dc]

        stg_marks = [0, 0]
        stg_i = [0]

        def stage_load_convert(dst, src):
            slot = stg_i[0] % 2
            stg_i[0] += 1
            st = stg[:, slot * 2048:(slot + 1) * 2048]
            ncols = int(np.prod(src.shape[1:]))
            W("sync", vS, stg_marks[slot])
            d = dma(st[:, 0:ncols], src, ctr=stgS[slot])
            W("vector", stgS[slot], d)
            m = vop(lambda e, a=dst, b=st[:, 0:ncols]: e.tensor_copy(a, b))
            stg_marks[slot] = m
            return m

        def finish_debug():
            for ctr in (vS, aS, tS, gS):
                W("sync", ctr, ctr.n)
            if upto < 4:
                d_dbg = dma(d_out[0:128, 0:96], modcol[:, :], ctr=oS)
                d_dbg = dma(d_out[0:128, 96:192], modp[:, :], ctr=oS)
                d_dbg = dma(d_out[0:128, 192:416], stg[:, 0:224], ctr=oS)
            if upto == 4 and dbg_x2[0] is not None:
                for b in range(16):
                    d_dbg = dma(d_out[b * 128:(b + 1) * 128, 0:1024],
                                dbg_x2[0][:, b * 1024:(b + 1) * 1024], ctr=oS)
            if dbg_h1[0] is not None:
                h1t = dbg_h1[0]
                nb = min(12, h1t.shape[1] // 2048)
                dob = d_out.bitcast(BF16)
                for b in range(nb):
                    d_dbg = dma(dob[256 + b * 128: 256 + (b + 1) * 128, :],
                                h1t[:, b * 2048:(b + 1) * 2048], ctr=oS)
            W("sync", oS, d_dbg)
            blk = en(nc.Block())
            for eng in ENGS:
                def runner(e, _fns=tuple(ops[eng])):
                    for f in _fns:
                        f(e)
                getattr(blk, eng)(runner)

        def gate(k):
            if upto < k:
                finish_debug()
                raise _Done()

        for j, val in enumerate(cvals):
            gop(lambda e, v=float(val), jj=j: e.memset(consts[:, jj:jj + 1], v),
                inc=False)
        gop(lambda e: e.memset(modp[:, :], 0.0), inc=False)
        gop(lambda e: e.memset(onescb[:, :], 1.0), inc=False)
        gop(lambda e: e.memset(ones1b[:, :], 1.0), inc=False)
        gop(lambda e: e.memset(ones1f[:, :], 1.0), inc=False)
        gop(lambda e: e.memset(idb[:, :], 1.0))
        W("gpsimd", gS, gS.n)
        gop(lambda e: e.affine_select(idb[:, :], idb[:, :],
            pattern=[[-1, 128]], compare_op=OP.is_equal, fill=0.0,
            base=0, channel_multiplier=1))
        W("gpsimd", gS, gS.n)
        gop(lambda e: e.memset(m_le[:, :], 1.0))
        W("gpsimd", gS, gS.n)
        gop(lambda e: e.affine_select(m_le[:, :], m_le[:, :],
            pattern=[[-1, 128]], compare_op=OP.is_ge, fill=0.0,
            base=0, channel_multiplier=1))
        W("gpsimd", gS, gS.n)
        gop(lambda e: e.memset(m_ge[:, :], 1.0))
        W("gpsimd", gS, gS.n)
        g_setup = gop(lambda e: e.affine_select(m_ge[:, :], m_ge[:, :],
            pattern=[[1, 128]], compare_op=OP.is_ge, fill=0.0,
            base=0, channel_multiplier=-1))
        for eng in ("vector", "scalar", "tensor"):
            W(eng, gS, g_setup)

        dma(cTs[:, :], d_cT, ctr=d0)
        dma(gscs[:, :], d_gsc, ctr=d0)
        dma(uscs[:, :], d_usc, ctr=d0)
        dma(dscs[:, :], d_dsc, ctr=d0)
        d_small = dma(halocol[:, :], d_halo, ctr=d0)
        for eng in ("vector", "scalar", "tensor"):
            W(eng, d0, d_small)

        gate(1)
        a_siluc = aop(lambda e: e.activation(silucT[:, :], cTs[:, :], AF.Silu))
        W("tensor", aS, a_siluc)

        s0 = contextlib.ExitStack()
        wmb = [s0.enter_context(nc.sbuf_tensor(f"wmb{i}", [128, 16 * 512], F32))
               for i in range(2)]
        mrl = s0.enter_context(nc.sbuf_tensor("mrl", [1, 3 * 512], F32))
        mclh = s0.enter_context(nc.sbuf_tensor("mclh", [128, 96], BF16))
        mcll = s0.enter_context(nc.sbuf_tensor("mcll", [128, 96], BF16))
        g96 = s0.enter_context(nc.sbuf_tensor("g96", [96, 128], F32))
        ghi = s0.enter_context(nc.sbuf_tensor("ghi", [96, 128], BF16))
        glo = s0.enter_context(nc.sbuf_tensor("glo", [96, 128], BF16))
        row_marks = [0, 0]
        wm_consumed = [0, 0]
        mod_store_done = [0]
        for nf in range(24):
            par = nf % 2
            buf = wmb[par]
            W("sync", tS, wm_consumed[par])
            d = None
            for dq in range(4):
                src = d_wmods[dq * 512:(dq + 1) * 512,
                              nf * 512:(nf + 1) * 512].rearrange(
                                  "(a p) f -> p a f", p=128)
                d = dma(buf[:, dq * 2048:(dq + 1) * 2048], src, ctr=wmS[par])
            W("tensor", wmS[par], d)
            ps = PB[par]
            W("tensor", vS, row_marks[par])
            t_row = None
            for dc in range(DC):
                t_row = mm(ps[0:1, :], silucT[:, dc:dc + 1],
                           buf[:, dc * 512:(dc + 1) * 512],
                           start=(dc == 0), stop=(dc == DC - 1),
                           inc=(dc == DC - 1))
            wm_consumed[par] = t_row
            W("vector", tS, t_row)
            W("vector", msS, mod_store_done[0])
            mrf = mrl[0:1, 0:512]
            v_row = vop(lambda e, a=mrf, b=ps[0:1, :]: e.tensor_copy(a, b))
            row_marks[par] = v_row
            W("sync", vS, v_row)
            mod_store_done[0] = dma(d_mrows[nf * 4:(nf + 1) * 4, :], mrf,
                                    ctr=msS)
        W("sync", msS, msS.n)
        d_g = dma(g96[:, :], d_mrows, ctr=mcS)
        W("vector", mcS, d_g)
        vop(lambda e: e.tensor_copy(ghi[:, :], g96[:, :]))
        W("vector", vS, vS.n)
        v_glo = vop(lambda e: e.tensor_sub(glo[:, :], g96[:, :], ghi[:, :]))
        W("sync", vS, v_glo)
        dma(d_mhi, ghi[:, :], ctr=mcS)
        dma(d_mlo, glo[:, :], ctr=mcS)
        W("sync", mcS, mcS.n)
        emit("sync", lambda e: e.dma_start_transpose(mclh[:, :], d_mhi)
             .then_inc(mcS.h, 16))
        mcS.bump(16)
        emit("sync", lambda e: e.dma_start_transpose(mcll[:, :], d_mlo)
             .then_inc(mcS.h, 16))
        d_tr = mcS.bump(16)
        W("vector", mcS, d_tr)
        vop(lambda e: e.tensor_tensor(modcol[:, :], mclh[:, :], mcll[:, :],
                                      OP.add))
        s0.close()
        W("vector", vS, vS.n)
        vop(lambda e: e.tensor_scalar_add(modp[:, 16:32], modcol[:, 16:32], 1.0))
        vop(lambda e: e.tensor_scalar_add(modp[:, 64:80], modcol[:, 64:80], 1.0))
        W("vector", vS, vS.n)
        for gi, s_ in enumerate((32, 80)):
            tc0 = gi * 16
            vop(lambda e, s=s_, t=tc0: e.tensor_single_scalar(
                stg[:, 64 + t:64 + t + 16], modcol[:, s:s + 16], 0.0,
                OP.is_ge))
            W("vector", vS, vS.n)
            vop(lambda e, t=tc0: e.tensor_scalar_mul(
                stg[:, 96 + t:96 + t + 16], stg[:, 64 + t:64 + t + 16],
                2.0))
            W("vector", vS, vS.n)
            vop(lambda e, t=tc0: e.tensor_scalar_add(
                stg[:, 128 + t:128 + t + 16], stg[:, 96 + t:96 + t + 16],
                -1.0))
            W("vector", vS, vS.n)
            W("scalar", vS, vS.n)
            a_abs = aop(lambda e, s=s_, t=tc0: e.activation(
                stg[:, t:t + 16], modcol[:, s:s + 16], AF.Abs))
            W("vector", aS, a_abs)
            vop(lambda e, t=tc0: e.tensor_scalar_max(
                stg[:, 32 + t:32 + t + 16], stg[:, t:t + 16], 0.25))
            W("vector", vS, vS.n)
            vop(lambda e, s=s_, t=tc0: e.tensor_mul(
                modp[:, s:s + 16], stg[:, 128 + t:128 + t + 16],
                stg[:, 32 + t:32 + t + 16]))
            W("vector", vS, vS.n)
        v_mod = vop(lambda e: e.tensor_mul(dscgb[:, :], dscs[:, :],
                                           modp[:, 80:96]))
        W("scalar", vS, v_mod)
        W("tensor", vS, v_mod)
        stg_marks[0] = v_mod
        stg_marks[1] = v_mod

        h1 = en(nc.sbuf_tensor("h1", [128, DC * TKV], BF16))
        if upto == 2:
            dbg_h1[0] = h1

        def h1_at(dc, t0, n):
            return h1[:, dc * TKV + t0: dc * TKV + t0 + n]

        def ln_phase(src_at, dst_at, nstrips, scale_c, bias_c, lntmp):
            lnb = lntmp.ap().bitcast(BF16)
            stat_slot = [0, 0]
            act_slot = [0, 0]
            bcast_war = [0, 0]
            for s in range(nstrips):
                par = s % 2
                psum_s = PB[4 + 2 * par]
                psum_q = PB[5 + 2 * par]
                pbm = PB[0 + 2 * par]
                pbr = PB[1 + 2 * par]
                for dc in range(DC):
                    slot = dc % 2
                    xb = lnb[:, 4096 + slot * 1024: 4096 + slot * 1024 + 512]
                    xq = lnb[:, 4608 + slot * 1024: 4608 + slot * 1024 + 512]
                    W("vector", tS, stat_slot[slot])
                    vop(lambda e, a=xb, b=src_at(dc, s):
                        e.tensor_copy(a, b))
                    v_c = vfence()
                    W("scalar", vS, v_c)
                    aop(lambda e, a=xq, b=xb: e.activation(a, b, AF.Square))
                    a_q = afence()
                    W("tensor", vS, v_c)
                    mm(psum_s[0:1, :], onescb[:, :], xb,
                       start=(dc == 0), stop=(dc == DC - 1))
                    W("tensor", aS, a_q)
                    stat_slot[slot] = mm(psum_q[0:1, :], onescb[:, :], xq,
                                         start=(dc == 0),
                                         stop=(dc == DC - 1), inc=True)
                W("vector", tS, stat_slot[(DC - 1) % 2])
                mrow = rows[0:1, 0:512]
                qrow = rows[0:1, 512:1024]
                vrow = rows[0:1, 1024:1536]
                rrow = rows[0:1, 1536:2048]
                vop(lambda e, a=mrow, b=psum_s[0:1, :]:
                    e.tensor_scalar_mul(a, b, 1.0 / D))
                vop(lambda e, a=qrow, b=psum_q[0:1, :]:
                    e.tensor_scalar_mul(a, b, 1.0 / D))
                W("vector", vS, vS.n)
                vop(lambda e, a=rrow, b=mrow: e.tensor_mul(a, b, b))
                W("vector", vS, vS.n)
                v_var = vop(lambda e, a=vrow, b=qrow, c=rrow:
                            e.tensor_sub(a, b, c))
                W("scalar", vS, v_var)
                aop(lambda e, a=rrow, b=vrow: e.activation(a, b, AF.Ln,
                                                           bias=EPS))
                W("scalar", aS, aS.n)
                a_rstd = aop(lambda e, a=qrow, b=rrow: e.activation(
                    a, b, AF.Exp, scale=-0.5))
                W("tensor", aS, a_rstd)
                W("tensor", vS, bcast_war[par])
                mm(pbm[:, :], ones1f[0:1, :], mrow, start=True, stop=True)
                t_bc = mm(pbr[:, :], ones1f[0:1, :], qrow, start=True,
                          stop=True, inc=True)
                W("vector", tS, t_bc)
                for dc in range(DC):
                    slot = dc % 2
                    t1 = lntmp[:, slot * 1024: slot * 1024 + 512]
                    t2 = lntmp[:, slot * 1024 + 512: slot * 1024 + 1024]
                    vop(lambda e, a=t1, b=src_at(dc, s), c=pbm[:, :]:
                        e.tensor_sub(a, b, c))
                    W("vector", vS, vS.n)
                    vop(lambda e, a=t2, b=t1, c=pbr[:, :]:
                        e.tensor_mul(a, b, c))
                    W("vector", vS, vS.n)
                    act_slot[slot] = vop(
                        lambda e, a=dst_at(dc, s), b=t2, sc=scale_c(dc),
                        bi=bias_c(dc): e.tensor_scalar(a, b, sc, bi,
                                                       OP.mult, OP.add))
                bcast_war[par] = act_slot[(DC - 1) % 2]

        gate(2)
        s1s = contextlib.ExitStack()
        Xb = s1s.enter_context(nc.sbuf_tensor("Xb", [128, DC * TOK], F32))
        Xh = s1s.enter_context(nc.sbuf_tensor("Xh", [128, DC * HALO], F32))
        lntmp1 = s1s.enter_context(nc.sbuf_tensor("lntmp1", [128, 3072], F32))
        W("sync", vS, vS.n)
        d1 = dma(Xh[:, :], d_xh.rearrange("(a p) t -> p a t", p=128), ctr=dXh)
        d2 = dma(Xb[:, :], d_xo.rearrange("(a p) t -> p a t", p=128), ctr=dXo)
        W("vector", dXh, d1)
        ln_phase(lambda dc, s: Xh[:, dc * HALO: dc * HALO + 512],
                 lambda dc, s: h1_at(dc, 0, 512), 1, ca1_c, sa_c, lntmp1)
        W("vector", dXo, d2)
        ln_phase(lambda dc, s: Xb[:, dc * TOK + s * 512:
                                  dc * TOK + s * 512 + 512],
                 lambda dc, s: h1_at(dc, HALO + s * 512, 512),
                 2, ca1_c, sa_c, lntmp1)
        ln1_done = vS.n
        s1s.close()

        gate(3)
        sA = contextlib.ExitStack()
        obuf = sA.enter_context(nc.sbuf_tensor("obuf", [128, NH * TOK], BF16))
        if upto == 3:
            dbg_h1[0] = obuf
        s23 = contextlib.ExitStack()
        wgq = s23.enter_context(nc.sbuf_tensor("wgq", [128, DC * 512], BF16))
        wgk = s23.enter_context(nc.sbuf_tensor("wgk", [128, DC * 512], BF16))
        wgv = s23.enter_context(nc.sbuf_tensor("wgv", [128, DC * 512], BF16))
        qg = s23.enter_context(nc.sbuf_tensor("qg", [128, 4 * TOK], BF16))
        kg = s23.enter_context(nc.sbuf_tensor("kg", [128, 4 * TKV], BF16))
        vg = s23.enter_context(nc.sbuf_tensor("vg", [128, 12 * 512], BF16))
        pT = s23.enter_context(nc.sbuf_tensor("pT", [128, 8 * 512], BF16))
        nmrow = s23.enter_context(nc.sbuf_tensor("nmrow", [1, 512], BF16))
        nmcols = s23.enter_context(nc.sbuf_tensor("nmcols", [128, 6], F32))
        rlBs = s23.enter_context(nc.sbuf_tensor("rlBs", [128, 512], F32))

        W("tensor", vS, ln1_done)

        def wload(dst, col0, g, consumed_mark):
            W("sync", tS, consumed_mark)
            cm = None
            for dq in range(4):
                src = d_wqkvT[dq * 512:(dq + 1) * 512,
                              col0 + g * 512: col0 + (g + 1) * 512] \
                    .rearrange("(a p) f -> p a f", p=128)
                cm = stage_load_convert(dst[:, dq * 2048:(dq + 1) * 2048], src)
            return cm

        kcw = [128, 256, 384, 512, 512, 384, 256, 128]
        kcq0 = [0, 0, 0, 0, 0, 128, 256, 384]
        kcpk = [(0, 0), (1, 0), (2, 0), (3, 0), (4, 0), (5, 0),
                (1, 256), (0, 128)]

        kgrp = int(os.environ.get("KGRP", "4"))
        g_weights_consumed = [0]
        evac_q = [0, 0]
        evac_k = [0, 0]
        evac_v = [0, 0]
        attn_last_vo = [0]
        attn_last_exp = [0]

        for g in range(kgrp):
            wload(wgq, 0, g, g_weights_consumed[0])
            wload(wgk, D, g, g_weights_consumed[0])
            wload(wgv, 2 * D, g, g_weights_consumed[0])
            cv = vfence()
            W("tensor", vS, cv)

            if ksub == "onemm" and g == 0:
                mm(PB[0][:, :], wgq[:, 0:128], h1_at(0, HALO, 512),
                   start=True, stop=True, inc=True)
                t1mm = tS.n
                W("vector", tS, t1mm)
                vop(lambda e: e.tensor_copy(qg[:, 0:512], PB[0][:, :]))
            if ksub in ("dcloop", "dcevac", "dcevacf", "dcevaca") and g == 0:
                for dc in range(DC):
                    wsl = wgq[:, dc * 512: dc * 512 + 128]
                    mm(PB[0][:, :], wsl, h1_at(dc, HALO, 512),
                       start=(dc == 0), stop=(dc == DC - 1))
                    mm(PB[1][:, :], wsl, h1_at(dc, HALO + 512, 512),
                       start=(dc == 0), stop=(dc == DC - 1),
                       inc=(dc == DC - 1))
                t1mm = tS.n
                W("vector", tS, t1mm)
                if ksub == "dcevac":
                    vop(lambda e: e.tensor_copy(qg[:, 0:512], PB[0][:, :]))
                    vop(lambda e: e.tensor_copy(qg[:, 512:1024], PB[1][:, :]))
                if ksub == "dcevacf":
                    vop(lambda e: e.tensor_copy(stg[:, 0:512], PB[0][:, :]))
                    vop(lambda e: e.tensor_copy(stg[:, 512:1024], PB[1][:, :]))
                if ksub == "dcevaca":
                    aop(lambda e: e.copy(qg[:, 0:512], PB[0][:, :]))
                    aop(lambda e: e.copy(qg[:, 512:1024], PB[1][:, :]))
            nfs_q = 0 if ksub in ("wload", "onemm", "dcloop", "dcevac", "dcevacf", "dcevaca") else int(os.environ.get("KFS", "4"))
            for fs in range(nfs_q):
                par = fs % 2
                p0, p1 = PB[par * 2], PB[par * 2 + 1]
                W("tensor", vS, evac_q[par])
                tq = None
                for dc in range(DC):
                    wsl = wgq[:, dc * 512 + fs * 128: dc * 512 + fs * 128 + 128]
                    mm(p0[:, :], wsl, h1_at(dc, HALO, 512),
                       start=(dc == 0), stop=(dc == DC - 1))
                    tq = mm(p1[:, :], wsl, h1_at(dc, HALO + 512, 512),
                            start=(dc == 0), stop=(dc == DC - 1),
                            inc=(dc == DC - 1))
                W("vector", tS, tq)
                vop(lambda e, a=qg[:, fs * TOK: fs * TOK + 512], b=p0[:, :]:
                    e.tensor_copy(a, b))
                evac_q[par] = vop(
                    lambda e, a=qg[:, fs * TOK + 512:(fs + 1) * TOK],
                    b=p1[:, :]: e.tensor_copy(a, b))
            for fs in (range(4) if ksub not in ("q", "wload", "onemm", "dcloop", "dcevac", "dcevacf", "dcevaca") else []):
                par = fs % 2
                pj = [PB[4 + par * 2], PB[5 + par * 2], PB[par * 2]]
                W("tensor", vS, evac_k[par])
                W("tensor", vS, evac_q[par])
                tk = None
                for dc in range(DC):
                    wsl = wgk[:, dc * 512 + fs * 128: dc * 512 + fs * 128 + 128]
                    for j in range(3):
                        tk = mm(pj[j][:, :], wsl, h1_at(dc, j * 512, 512),
                                start=(dc == 0), stop=(dc == DC - 1),
                                inc=(dc == DC - 1 and j == 2))
                W("vector", tS, tk)
                vk = None
                for j in range(3):
                    vk = vop(lambda e, a=kg[:, fs * TKV + j * 512:
                                            fs * TKV + (j + 1) * 512],
                             b=pj[j][:, :]: e.tensor_copy(a, b))
                evac_k[par] = vk
                evac_q[par] = vk
            W("tensor", vS, evac_k[1])
            for tc in (range(12) if ksub not in ("q", "qk", "wload", "onemm", "dcloop", "dcevac", "dcevacf", "dcevaca") else []):
                par = tc % 2
                pv = PB[6 + par]
                W("tensor", vS, evac_v[par])
                tv = None
                for dc in range(DC):
                    tv = mm(pv[:, :], h1_at(dc, tc * 128, 128),
                            wgv[:, dc * 512: dc * 512 + 512],
                            start=(dc == 0), stop=(dc == DC - 1),
                            inc=(dc == DC - 1))
                W("vector", tS, tv)
                evac_v[par] = vop(
                    lambda e, a=vg[:, tc * 512:(tc + 1) * 512],
                    b=pv[:, :]: e.tensor_copy(a, b))
            g_weights_consumed[0] = tS.n
            W("tensor", vS, max(evac_v))
            W("tensor", vS, max(evac_k))

            n_att = [99]
            if ksub in ("prod", "q", "qk", "qkv", "wload", "onemm", "dcloop", "dcevac", "dcevacf", "dcevaca"):
                n_att = [0]
            elif ksub.startswith("att"):
                n_att = [int(ksub[3:])]
            att_count = [0]
            for hh in range(4):
                h = g * 4 + hh
                for n in range(2):
                    if att_count[0] >= n_att[0]:
                        continue
                    att_count[0] += 1
                    qwin = qg[:, hh * TOK + n * 512: hh * TOK + n * 512 + 512]
                    kwin0 = hh * TKV + n * 512

                    def k_at(ka, w, _k0=kwin0):
                        return kg[:, _k0 + ka: _k0 + ka + w]

                    W("tensor", vS, attn_last_vo[0])
                    W("tensor", aS, attn_last_exp[0])
                    red_marks = []
                    for qc in range(4):
                        pa, pb_ = PB[qc * 2], PB[qc * 2 + 1]
                        mm(pa[:, :], qwin[:, qc * 128:(qc + 1) * 128],
                           k_at(0, 512), start=True, stop=True)
                        tqc = mm(pb_[:, :], qwin[:, qc * 128:(qc + 1) * 128],
                                 k_at(512, 512), start=True, stop=True,
                                 inc=True)
                        W("vector", tS, tqc)
                        nmc = nmcols[:, qc:qc + 1]
                        vop(lambda e, a=nmcols[:, 4:5], b=pa[:, :]:
                            e.reduce_max(a, b, AX, negate=True))
                        vop(lambda e, a=nmcols[:, 5:6], b=pb_[:, :]:
                            e.reduce_max(a, b, AX, negate=True))
                        W("vector", vS, vS.n)
                        red_marks.append(vop(
                            lambda e, a=nmc, b=nmcols[:, 4:5],
                            c=nmcols[:, 5:6]:
                            e.tensor_tensor(a, b, c, OP.min)))
                        W("vector", vS, vS.n)
                    dsc_marks = []
                    for qc in range(4):
                        dsc_marks.append(vop(
                            lambda e, o=diagsc[:, :], i=idb[:, :],
                            sc=nmcols[:, qc:qc + 1]:
                            e.tensor_scalar_mul(o, i, sc)))
                        W("tensor", vS, dsc_marks[-1])
                        t_nm = mm(PB[6][0:1, qc * 128:(qc + 1) * 128],
                                  onescb[:, :], diagsc[:, :],
                                  start=(qc == 0), stop=(qc == 3),
                                  inc=True)
                        W("vector", tS, t_nm)
                    v_nmr = vop(lambda e, a=nmrow[0:1, :],
                                b=PB[6][0:1, 0:512]: e.tensor_copy(a, b))
                    W("tensor", vS, v_nmr)
                    first_in_bank = set()
                    for kc in range(8):
                        bnk, c0 = kcpk[kc]
                        w_ = kcw[kc]
                        mm(PB[bnk][:, c0:c0 + w_], k_at(kc * 128, 128),
                           qwin[:, kcq0[kc]:kcq0[kc] + w_],
                           start=(bnk not in first_in_bank), stop=False)
                        first_in_bank.add(bnk)
                    t_s2 = None
                    for kc in range(8):
                        bnk, c0 = kcpk[kc]
                        w_ = kcw[kc]
                        t_s2 = mm(PB[bnk][:, c0:c0 + w_], ones1b[0:1, :],
                                  nmrow[0:1, kcq0[kc]:kcq0[kc] + w_],
                                  start=False, stop=(kc >= 2), inc=(kc == 7))
                    W("scalar", tS, t_s2)
                    a_exp = None
                    for kc in range(8):
                        bnk, c0 = kcpk[kc]
                        w_ = kcw[kc]
                        a_exp = aop(lambda e,
                                    o=pT[:, kc * 512: kc * 512 + w_],
                                    i=PB[bnk][:, c0:c0 + w_]:
                                    e.activation(o, i, AF.Exp, scale=SCALE))
                    attn_last_exp[0] = a_exp
                    W("vector", aS, a_exp)
                    v_m = None
                    for kc in range(8):
                        if kc < 4:
                            eb = kc * 128
                            msk = m_le
                        else:
                            eb = 0
                            msk = m_ge
                        v_m = vop(lambda e, a=pT[:, kc * 512 + eb:
                                                 kc * 512 + eb + 128],
                                  b=msk[:, :]: e.tensor_mul(a, a, b))
                        if n == 0 and kc < 4:
                            W("vector", vS, vS.n)
                            v_m = vop(lambda e, a=pT[:, kc * 512:
                                                     kc * 512 + kcw[kc]]:
                                      e.tensor_scalar_mul(a, a,
                                                          halocol[:, :]))
                    W("tensor", vS, v_m)
                    kc_order = [3, 4, 2, 1, 0, 5, 6, 7]
                    t_l = None
                    for ki, kc in enumerate(kc_order):
                        t_l = mm(PB[7][0:1, kcq0[kc]:kcq0[kc] + kcw[kc]],
                                 onescb[:, :],
                                 pT[:, kc * 512: kc * 512 + kcw[kc]],
                                 start=(ki == 0), stop=(ki == 7),
                                 inc=(ki == 7))
                    t_pv = None
                    for ki, kc in enumerate(kc_order):
                        tcg = n * 4 + kc
                        vsl = vg[:, tcg * 512 + hh * 128:
                                 tcg * 512 + hh * 128 + 128]
                        t_pv = mm(PB[6][:, kcq0[kc]:kcq0[kc] + kcw[kc]], vsl,
                                  pT[:, kc * 512: kc * 512 + kcw[kc]],
                                  start=(ki == 0), stop=(ki == 7),
                                  inc=(ki == 7))
                    W("scalar", tS, t_l)
                    lr = rows[0:1, 0:512]
                    lr2 = rows[0:1, 512:1024]
                    aop(lambda e, a=lr, b=PB[7][0:1, 0:512]:
                        e.activation(a, b, AF.Ln))
                    W("scalar", aS, aS.n)
                    a_rl = aop(lambda e, a=lr2, b=lr: e.activation(
                        a, b, AF.Exp, scale=-1.0))
                    W("tensor", aS, a_rl)
                    t_rlb = mm(PB[7][:, :], ones1f[0:1, :], lr2,
                               start=True, stop=True, inc=True)
                    W("vector", tS, t_rlb)
                    W("vector", tS, t_pv)
                    vop(lambda e, a=rlBs[:, :], b=PB[7][:, :]:
                        e.tensor_copy(a, b))
                    W("vector", vS, vS.n)
                    attn_last_vo[0] = vop(
                        lambda e, a=obuf[:, h * TOK + n * 512:
                                         h * TOK + n * 512 + 512],
                        b=PB[6][:, :], c=rlBs[:, :]: e.tensor_mul(a, b, c))
        W("tensor", vS, attn_last_vo[0])
        s23.close()

        gate(4)
        s45 = contextlib.ExitStack()
        X2 = s45.enter_context(nc.sbuf_tensor("X2", [128, DC * TOK], F32))
        if upto == 4:
            dbg_x2[0] = X2
        s4 = contextlib.ExitStack()
        wo = [s4.enter_context(nc.sbuf_tensor(f"wo{i}", [128, DC * 256], BF16))
              for i in range(2)]
        xt = s4.enter_context(nc.sbuf_tensor("xt", [128, 4 * 512], F32))
        xt_marks = [0, 0, 0, 0]
        n_xt = [0]
        x2_marks = [0, 0]
        wo_consumed = [0, 0]
        wo_ready = [0, 0]

        def load_wo(ocb):
            par = ocb % 2
            W("sync", tS, wo_consumed[par])
            for dq in range(4):
                src = d_woutT[dq * 512:(dq + 1) * 512,
                              ocb * 256:(ocb + 1) * 256] \
                    .rearrange("(a p) f -> p a f", p=128)
                stage_load_convert(wo[par][:, dq * 1024:(dq + 1) * 1024], src)
            wo_ready[par] = vfence()

        load_wo(0)
        for ocb in range(8):
            par = ocb % 2
            if ocb < 7:
                load_wo(ocb + 1)
            W("tensor", vS, wo_ready[par])
            for ec in range(2):
                e_g = ocb * 2 + ec
                ppar = ec % 2
                p0, p1 = PB[ppar * 2], PB[ppar * 2 + 1]
                sl0, sl1 = n_xt[0] % 4, (n_xt[0] + 1) % 4
                n_xt[0] += 2
                x0 = xt[:, sl0 * 512:(sl0 + 1) * 512]
                x1 = xt[:, sl1 * 512:(sl1 + 1) * 512]
                W("sync", vS, xt_marks[sl0])
                dx0 = dma(x0, d_xo[e_g * 128:(e_g + 1) * 128, 0:512],
                          ctr=xtS[sl0])
                W("sync", vS, xt_marks[sl1])
                dx1 = dma(x1, d_xo[e_g * 128:(e_g + 1) * 128, 512:1024],
                          ctr=xtS[sl1])
                W("tensor", vS, x2_marks[ppar])
                t_ec = None
                for kf in range(DC):
                    wsl = wo[par][:, kf * 256 + ec * 128:
                                  kf * 256 + ec * 128 + 128]
                    mm(p0[:, :], wsl, obuf[:, kf * TOK: kf * TOK + 512],
                       start=(kf == 0), stop=(kf == DC - 1))
                    t_ec = mm(p1[:, :], wsl,
                              obuf[:, kf * TOK + 512:(kf + 1) * TOK],
                              start=(kf == 0), stop=(kf == DC - 1),
                              inc=(kf == DC - 1))
                W("vector", tS, t_ec)
                W("vector", xtS[sl0], dx0)
                W("vector", xtS[sl1], dx1)
                if ksub == "wraw":
                    xt_marks[sl0] = vop(
                        lambda e, o=X2[:, e_g * TOK: e_g * TOK + 512],
                        p=p0[:, :]: e.tensor_copy(o, p))
                    v_x2 = vop(
                        lambda e, o=X2[:, e_g * TOK + 512:(e_g + 1) * TOK],
                        p=p1[:, :]: e.tensor_copy(o, p))
                else:
                    xt_marks[sl0] = vop(
                        lambda e, o=X2[:, e_g * TOK: e_g * TOK + 512],
                        p=p0[:, :], sc=ga_c(e_g), x=x0:
                        e.scalar_tensor_tensor(o, p, sc, x, OP.mult, OP.add))
                    v_x2 = vop(
                        lambda e, o=X2[:, e_g * TOK + 512:(e_g + 1) * TOK],
                        p=p1[:, :], sc=ga_c(e_g), x=x1:
                        e.scalar_tensor_tensor(o, p, sc, x, OP.mult, OP.add))
                xt_marks[sl1] = v_x2
                x2_marks[ppar] = v_x2
            wo_consumed[par] = tS.n
        s4.close()

        gate(5)
        s5 = contextlib.ExitStack()
        lntmp2 = s5.enter_context(nc.sbuf_tensor("lntmp2", [128, 3072], F32))
        ln_phase(lambda dc, s: X2[:, dc * TOK + s * 512:
                                  dc * TOK + s * 512 + 512],
                 lambda dc, s: h1_at(dc, HALO + s * 512, 512),
                 2, cb1_c, sb_c, lntmp2)
        s5.close()
        if upto == 5:
            dbg_h1[0] = h1
        ln2_done = vS.n
        W("sync", vS, vS.n)
        d_spill = dma(d_x2.rearrange("(a p) t -> p a t", p=128), X2[:, :],
                      ctr=spS)
        W("vector", spS, d_spill)
        s45.close()
        sA.close()

        h2_at = lambda dc, n: h1_at(dc, HALO + n * 512, 512)
        W("tensor", vS, ln2_done)

        gate(6)
        fuse = en(nc.sbuf_tensor("fuse", [128, FC * TOK], BF16))
        s6 = contextlib.ExitStack()
        gwb = [s6.enter_context(nc.sbuf_tensor(f"gwb{i}", [128, DC * 512], FP8))
               for i in range(2)]
        uwb = [s6.enter_context(nc.sbuf_tensor(f"uwb{i}", [128, DC * 512], FP8))
               for i in range(2)]
        gt = s6.enter_context(nc.sbuf_tensor("gt", [128, 2 * 512], BF16))
        ut = s6.enter_context(nc.sbuf_tensor("ut", [128, 2 * 512], BF16))
        vf_marks = [0, 0]
        gu_consumed = [0, 0]
        gu_ready = [0, 0]

        def load_gu(fb):
            par = fb % 2
            W("sync", tS, gu_consumed[par])
            for dq in range(4):
                src = d_gw[dq * 512:(dq + 1) * 512,
                           fb * 512:(fb + 1) * 512] \
                    .rearrange("(a p) f -> p a f", p=128)
                stage_load_convert(gwb[par][:, dq * 2048:(dq + 1) * 2048], src)
            for dq in range(4):
                src = d_uw[dq * 512:(dq + 1) * 512,
                           fb * 512:(fb + 1) * 512] \
                    .rearrange("(a p) f -> p a f", p=128)
                stage_load_convert(uwb[par][:, dq * 2048:(dq + 1) * 2048], src)
            gu_ready[par] = vfence()

        load_gu(0)
        for fb in range(12):
            par = fb % 2
            if fb < 11:
                load_gu(fb + 1)
            W("tensor", vS, gu_ready[par])
            for f2 in range(4):
                fc = fb * 4 + f2
                ppar = f2 % 2
                pg = [PB[ppar * 4 + 0], PB[ppar * 4 + 1]]
                pu = [PB[ppar * 4 + 2], PB[ppar * 4 + 3]]
                W("tensor", vS, vf_marks[ppar])
                t_fc = None
                for dc in range(DC):
                    gsl = gwb[par][:, dc * 512 + f2 * 128:
                                   dc * 512 + f2 * 128 + 128]
                    usl = uwb[par][:, dc * 512 + f2 * 128:
                                   dc * 512 + f2 * 128 + 128]
                    mm(pg[0][:, :], gsl, h2_at(dc, 0),
                       start=(dc == 0), stop=(dc == DC - 1))
                    mm(pg[1][:, :], gsl, h2_at(dc, 1),
                       start=(dc == 0), stop=(dc == DC - 1))
                    mm(pu[0][:, :], usl, h2_at(dc, 0),
                       start=(dc == 0), stop=(dc == DC - 1))
                    t_fc = mm(pu[1][:, :], usl, h2_at(dc, 1),
                              start=(dc == 0), stop=(dc == DC - 1),
                              inc=(dc == DC - 1))
                W("scalar", tS, t_fc)
                W("vector", tS, t_fc)
                W("scalar", vS, vf_marks[1 - ppar])
                v_f = None
                for n in range(2):
                    a_g = aop(lambda e, o=gt[:, n * 512:(n + 1) * 512],
                              i=pg[n][:, :], sc=gscs[:, fc:fc + 1]:
                              e.activation(o, i, AF.Silu, scale=sc))
                    vop(lambda e, o=ut[:, n * 512:(n + 1) * 512],
                        i=pu[n][:, :], sc=uscs[:, fc:fc + 1]:
                        e.tensor_scalar_mul(o, i, sc))
                    W("vector", vS, vS.n)
                    W("vector", aS, a_g)
                    v_f = vop(lambda e, o=fuse[:, fc * TOK + n * 512:
                                               fc * TOK + n * 512 + 512],
                              a=gt[:, n * 512:(n + 1) * 512],
                              b=ut[:, n * 512:(n + 1) * 512]:
                              e.tensor_mul(o, a, b))
                vf_marks[ppar] = v_f
            gu_consumed[par] = tS.n
        s6.close()

        gate(7)
        s7 = contextlib.ExitStack()
        dwb = [s7.enter_context(nc.sbuf_tensor(f"dwb{i}", [128, FC * 256], FP8))
               for i in range(2)]
        x2t = s7.enter_context(nc.sbuf_tensor("x2t", [128, 2 * 512], F32))
        ob = s7.enter_context(nc.sbuf_tensor("ob", [128, 2 * 512], F32))
        x2t_marks = [0, 0]
        dn_marks = [0, 0]
        dw_consumed = [0, 0]
        dw_ready = [0, 0]
        W("sync", spS, d_spill)

        def load_dw(ecb):
            par = ecb % 2
            W("sync", tS, dw_consumed[par])
            for fq in range(6):
                src = d_dw[fq * 1024:(fq + 1) * 1024,
                           ecb * 256:(ecb + 1) * 256] \
                    .rearrange("(a p) f -> p a f", p=128)
                stage_load_convert(dwb[par][:, fq * 2048:(fq + 1) * 2048], src)
            dw_ready[par] = vfence()

        load_dw(0)
        for ecb in range(8):
            par = ecb % 2
            if ecb < 7:
                load_dw(ecb + 1)
            W("tensor", vS, dw_ready[par])
            for e2 in range(2):
                ec = ecb * 2 + e2
                ppar = e2 % 2
                p0, p1 = PB[ppar * 2], PB[ppar * 2 + 1]
                W("tensor", vS, dn_marks[ppar])
                xs0, xs1 = 0, 1
                W("sync", vS, x2t_marks[xs0])
                dx0 = dma(x2t[:, 0:512], d_x2[ec * 128:(ec + 1) * 128, 0:512],
                          ctr=x2S[xs0])
                W("sync", vS, x2t_marks[xs1])
                dx1 = dma(x2t[:, 512:1024],
                          d_x2[ec * 128:(ec + 1) * 128, 512:1024],
                          ctr=x2S[xs1])
                t_dn = None
                for fc in range(FC):
                    wsl = dwb[par][:, fc * 256 + e2 * 128:
                                   fc * 256 + e2 * 128 + 128]
                    mm(p0[:, :], wsl, fuse[:, fc * TOK: fc * TOK + 512],
                       start=(fc == 0), stop=(fc == FC - 1))
                    t_dn = mm(p1[:, :], wsl,
                              fuse[:, fc * TOK + 512:(fc + 1) * TOK],
                              start=(fc == 0), stop=(fc == FC - 1),
                              inc=(fc == FC - 1))
                W("vector", tS, t_dn)
                W("vector", x2S[xs0], dx0)
                W("vector", x2S[xs1], dx1)
                v_last = None
                for n in range(2):
                    osl = n % 2
                    W("vector", obS[osl], obS[osl].n)
                    v_last = vop(lambda e, o=ob[:, osl * 512:(osl + 1) * 512],
                                 p=(p0 if n == 0 else p1)[:, :],
                                 sc=dscgb[:, ec:ec + 1],
                                 x=x2t[:, n * 512:(n + 1) * 512]:
                                 e.scalar_tensor_tensor(o, p, sc, x,
                                                        OP.mult, OP.add))
                    if n == 1:
                        x2t_marks[0] = v_last
                        x2t_marks[1] = v_last
                    W("sync", vS, v_last)
                    dma(d_out[ec * 128:(ec + 1) * 128,
                              n * 512:(n + 1) * 512],
                        ob[:, osl * 512:(osl + 1) * 512], ctr=obS[osl])
                dn_marks[ppar] = v_last
            dw_consumed[par] = tS.n
        W("sync", obS[0], obS[0].n)
        W("sync", obS[1], obS[1].n)
        s7.close()

        blk = en(nc.Block())
        for eng in ENGS:
            def runner(e, _fns=tuple(ops[eng])):
                for f in _fns:
                    f(e)
            getattr(blk, eng)(runner)
    except _Done:
        pass

    return nc


_cache = {}


def kernel(L0_attn_x, L0_attn_w_qkv, L0_attn_w_out, L0_ffn_gate_w,
           L0_ffn_up_w, L0_ffn_down_w, L0_ffn_gate_scale, L0_ffn_up_scale,
           L0_ffn_down_scale, L0_c, L0_wmod):
    f32 = np.float32
    xT = np.ascontiguousarray(np.asarray(L0_attn_x, f32)[0].T)
    wqkvT = np.ascontiguousarray(np.asarray(L0_attn_w_qkv, f32).T)
    woutT = np.ascontiguousarray(np.asarray(L0_attn_w_out, f32).T)
    gw = np.ascontiguousarray(np.asarray(L0_ffn_gate_w, f32))
    uw = np.ascontiguousarray(np.asarray(L0_ffn_up_w, f32))
    dw = np.ascontiguousarray(np.asarray(L0_ffn_down_w, f32))
    gscT = np.ascontiguousarray(
        np.asarray(L0_ffn_gate_scale, f32).reshape(FC, 128).T)
    uscT = np.ascontiguousarray(
        np.asarray(L0_ffn_up_scale, f32).reshape(FC, 128).T)
    dscT = np.ascontiguousarray(
        np.asarray(L0_ffn_down_scale, f32).reshape(DC, 128).T)
    cT = np.ascontiguousarray(np.asarray(L0_c, f32).reshape(DC, 128).T)
    wmodc = np.ascontiguousarray(np.asarray(L0_wmod, f32))

    import os
    upto = int(os.environ.get("KUPTO", "99"))
    key = ("nc", upto)
    if key not in _cache:
        _cache[key] = build_nc(upto)
    nc = _cache[key]

    in_maps = []
    for c in range(NCORES):
        xo = np.ascontiguousarray(xT[:, c * TOK:(c + 1) * TOK])
        if c == 0:
            xhs = np.zeros((D, HALO), f32)
            hcol = np.zeros((128, 1), f32)
        else:
            xhs = np.ascontiguousarray(xT[:, c * TOK - HALO: c * TOK])
            hcol = np.ones((128, 1), f32)
        in_maps.append({
            "xo": xo, "xh": xhs, "wqkvT": wqkvT, "woutT": woutT,
            "gw": gw, "uw": uw, "dw": dw, "gscT": gscT, "uscT": uscT,
            "dscT": dscT, "cT": cT, "halo_on": hcol,
            "wmods": wmodc,
        })
    res = run_bass_kernel_spmd(nc, in_maps, core_ids=list(range(NCORES)))
    outT = np.concatenate([np.asarray(r["outT"]) for r in res.results],
                          axis=1)
    return np.ascontiguousarray(outT.T).reshape(1, 8192, D)


# version 02 (final)
# speedup vs baseline: 1.0285x; 1.0285x over previous
import contextlib

import numpy as np

import concourse.bass as bass
import concourse.mybir as mybir
from concourse.bass_utils import run_bass_kernel_spmd

F32 = mybir.dt.float32
BF16 = mybir.dt.bfloat16
FP8 = mybir.dt.float8e4
AX = mybir.AxisListType.X
AF = mybir.ActivationFunctionType
OP = mybir.AluOpType

NCORES = 8
D = 2048
DC = 16
TOK = 1024
HALO = 512
TKV = TOK + HALO
NH = 16
FI = 6144
FC = 48
SCALE = float(1.0 / np.sqrt(128.0))
EPS = 1e-5
ENGS = ("sync", "gpsimd", "vector", "scalar", "tensor")


class Ctr:
    def __init__(self, h):
        self.h = h
        self.n = 0

    def bump(self, k):
        self.n += k
        return self.n


def build_nc(upto=99):
    import os
    ksub = os.environ.get("KSUB", "")
    nc = bass.Bass("TRN2", target_bir_lowering=False,
                   detect_race_conditions=os.environ.get("KRACE", "1") == "1")

    d_xo = nc.dram_tensor("xo", [D, TOK], F32, kind="ExternalInput").ap()
    d_xh = nc.dram_tensor("xh", [D, HALO], F32, kind="ExternalInput").ap()
    d_wqkvT = nc.dram_tensor("wqkvT", [D, 3 * D], F32, kind="ExternalInput").ap()
    d_woutT = nc.dram_tensor("woutT", [D, D], F32, kind="ExternalInput").ap()
    d_gw = nc.dram_tensor("gw", [D, FI], F32, kind="ExternalInput").ap()
    d_uw = nc.dram_tensor("uw", [D, FI], F32, kind="ExternalInput").ap()
    d_dw = nc.dram_tensor("dw", [FI, D], F32, kind="ExternalInput").ap()
    d_gsc = nc.dram_tensor("gscT", [128, FC], F32, kind="ExternalInput").ap()
    d_usc = nc.dram_tensor("uscT", [128, FC], F32, kind="ExternalInput").ap()
    d_dsc = nc.dram_tensor("dscT", [128, DC], F32, kind="ExternalInput").ap()
    d_cT = nc.dram_tensor("cT", [128, DC], F32, kind="ExternalInput").ap()
    d_wmods = nc.dram_tensor("wmods", [D, 6 * D], F32, kind="ExternalInput").ap()
    d_halo = nc.dram_tensor("halo_on", [128, 1], F32, kind="ExternalInput").ap()
    d_out = nc.dram_tensor("outT", [D, TOK], F32, kind="ExternalOutput").ap()
    d_x2 = nc.dram_tensor("x2scr", [D, TOK], F32).ap()
    d_mhi = nc.dram_tensor("modscr_hi", [96, 128], BF16).ap()
    d_mlo = nc.dram_tensor("modscr_lo", [96, 128], BF16).ap()
    d_mrows = nc.dram_tensor("modrows", [96, 128], F32).ap()

    ops = {e: [] for e in ENGS}
    last_wait = {}
    dbg_h1 = [None]
    dbg_x2 = [None]

    class _Done(Exception):
        pass


    def emit(eng, fn):
        ops[eng].append(fn)

    try:
      with contextlib.ExitStack() as top:
        en = top.enter_context
        dS = Ctr(en(nc.semaphore("dS")))
        oS = Ctr(en(nc.semaphore("oS")))
        tS = Ctr(en(nc.semaphore("tS")))
        vS = Ctr(en(nc.semaphore("vS")))
        aS = Ctr(en(nc.semaphore("aS")))
        gS = Ctr(en(nc.semaphore("gS")))
        d0 = Ctr(en(nc.semaphore("d0")))
        wmS = [Ctr(en(nc.semaphore(f"wmS{i}"))) for i in range(2)]
        msS = Ctr(en(nc.semaphore("msS")))
        mcS = Ctr(en(nc.semaphore("mcS")))
        dXh = Ctr(en(nc.semaphore("dXh")))
        dXo = Ctr(en(nc.semaphore("dXo")))
        stgS = [Ctr(en(nc.semaphore(f"stgS{i}"))) for i in range(2)]
        xtS = [Ctr(en(nc.semaphore(f"xtS{i}"))) for i in range(4)]
        spS = Ctr(en(nc.semaphore("spS")))
        x2S = [Ctr(en(nc.semaphore(f"x2S{i}"))) for i in range(2)]
        obS = [Ctr(en(nc.semaphore(f"obS{i}"))) for i in range(2)]

        def W(eng, ctr, val):
            if val is None or val <= 0:
                return
            key = (eng, id(ctr))
            if last_wait.get(key, -1) >= val:
                return
            last_wait[key] = val
            emit(eng, lambda e, h=ctr.h, v=val: e.wait_ge(h, v))

        def dma(dst, src, ctr=dS):
            emit("sync", lambda e, a=dst, b=src, h=ctr.h: e.dma_start(
                out=a, in_=b).then_inc(h, 16))
            return ctr.bump(16)

        def mm(out, lhsT, rhs, start, stop, inc=False):
            if inc:
                emit("tensor", lambda e, o=out, l=lhsT, r=rhs, s=start,
                     p=stop, h=tS.h: e.matmul(o, l, r, start=s, stop=p)
                     .then_inc(h, 1))
                return tS.bump(1)
            emit("tensor", lambda e, o=out, l=lhsT, r=rhs, s=start,
                 p=stop: e.matmul(o, l, r, start=s, stop=p))
            return None

        def ldw(w):
            emit("tensor", lambda e, a=w: e.ldweights(a))

        def tmark():
            emit("tensor", lambda e, h=tS.h: e.nop().then_inc(h, 1))
            return tS.bump(1)

        def vop(fn, inc=True):
            if inc:
                emit("vector", lambda e, f=fn, h=vS.h: f(e).then_inc(h, 1))
                return vS.bump(1)
            emit("vector", fn)
            return None

        def aop(fn, inc=True):
            if inc:
                emit("scalar", lambda e, f=fn, h=aS.h: f(e).then_inc(h, 1))
                return aS.bump(1)
            emit("scalar", fn)
            return None

        def vfence():
            W("vector", vS, vS.n)
            return vop(lambda e: e.engine_nop())

        def afence():
            W("scalar", aS, aS.n)
            return aop(lambda e: e.copy(fsc[0:1, 0:1], consts[0:1, 0:1]))

        def gop(fn, inc=True):
            if inc:
                emit("gpsimd", lambda e, f=fn, h=gS.h: f(e).then_inc(h, 1))
                return gS.bump(1)
            emit("gpsimd", fn)
            return None

        consts = en(nc.sbuf_tensor("consts", [128, 4], F32))
        onescb = en(nc.sbuf_tensor("onescb", [128, 1], BF16))
        ones1b = en(nc.sbuf_tensor("ones1b", [1, 128], BF16))
        ones1f = en(nc.sbuf_tensor("ones1f", [1, 128], F32))
        idb = en(nc.sbuf_tensor("idb", [128, 128], BF16))
        diagsc = en(nc.sbuf_tensor("diagsc", [128, 128], BF16))


        m_le = en(nc.sbuf_tensor("m_le", [128, 128], BF16))
        m_ge = en(nc.sbuf_tensor("m_ge", [128, 128], BF16))
        halocol = en(nc.sbuf_tensor("halocol", [128, 1], F32))
        modcol = en(nc.sbuf_tensor("modcol", [128, 96], F32))
        modp = en(nc.sbuf_tensor("modp", [128, 96], F32))
        silucT = en(nc.sbuf_tensor("silucT", [128, DC], F32))
        cTs = en(nc.sbuf_tensor("cTs", [128, DC], F32))
        gscs = en(nc.sbuf_tensor("gscs", [128, FC], F32))
        uscs = en(nc.sbuf_tensor("uscs", [128, FC], F32))
        dscs = en(nc.sbuf_tensor("dscs", [128, DC], F32))
        dscgb = en(nc.sbuf_tensor("dscgb", [128, DC], F32))
        rows = en(nc.sbuf_tensor("rows", [1, 2048], F32))
        fsc = en(nc.sbuf_tensor("fsc", [1, 4], F32))
        stg = en(nc.sbuf_tensor("stg", [128, 2 * 2048], F32))
        PB = [en(nc.psum_tensor(f"PB{i}", [128, 512], F32)) for i in range(8)]

        cvals = [SCALE, -1.0, -0.5, EPS]
        for j, val in enumerate(cvals):
            nc.const_aps.register(F32, float(val), consts[:, j:j + 1])

        sa_c = lambda dc: modcol[:, 0 + dc:1 + dc]
        sb_c = lambda dc: modcol[:, 48 + dc:49 + dc]
        ca1_c = lambda dc: modp[:, 16 + dc:17 + dc]
        cb1_c = lambda dc: modp[:, 64 + dc:65 + dc]
        ga_c = lambda dc: modp[:, 32 + dc:33 + dc]

        stg_marks = [0, 0]
        stg_i = [0]

        def stage_load_convert(dst, src):
            slot = stg_i[0] % 2
            stg_i[0] += 1
            st = stg[:, slot * 2048:(slot + 1) * 2048]
            ncols = int(np.prod(src.shape[1:]))
            W("sync", vS, stg_marks[slot])
            d = dma(st[:, 0:ncols], src, ctr=stgS[slot])
            W("vector", stgS[slot], d)
            m = vop(lambda e, a=dst, b=st[:, 0:ncols]: e.tensor_copy(a, b))
            stg_marks[slot] = m
            return m

        def finish_debug():
            for ctr in (vS, aS, tS, gS):
                W("sync", ctr, ctr.n)
            if upto < 4:
                d_dbg = dma(d_out[0:128, 0:96], modcol[:, :], ctr=oS)
                d_dbg = dma(d_out[0:128, 96:192], modp[:, :], ctr=oS)
                d_dbg = dma(d_out[0:128, 192:416], stg[:, 0:224], ctr=oS)
            if upto == 4 and dbg_x2[0] is not None:
                for b in range(16):
                    d_dbg = dma(d_out[b * 128:(b + 1) * 128, 0:1024],
                                dbg_x2[0][:, b * 1024:(b + 1) * 1024], ctr=oS)
            if dbg_h1[0] is not None:
                h1t = dbg_h1[0]
                nb = min(12, h1t.shape[1] // 2048)
                dob = d_out.bitcast(BF16)
                for b in range(nb):
                    d_dbg = dma(dob[256 + b * 128: 256 + (b + 1) * 128, :],
                                h1t[:, b * 2048:(b + 1) * 2048], ctr=oS)
            W("sync", oS, d_dbg)
            blk = en(nc.Block())
            for eng in ENGS:
                def runner(e, _fns=tuple(ops[eng])):
                    for f in _fns:
                        f(e)
                getattr(blk, eng)(runner)

        def gate(k):
            if upto < k:
                finish_debug()
                raise _Done()

        for j, val in enumerate(cvals):
            gop(lambda e, v=float(val), jj=j: e.memset(consts[:, jj:jj + 1], v),
                inc=False)
        gop(lambda e: e.memset(modp[:, :], 0.0), inc=False)
        gop(lambda e: e.memset(onescb[:, :], 1.0), inc=False)
        gop(lambda e: e.memset(ones1b[:, :], 1.0), inc=False)
        gop(lambda e: e.memset(ones1f[:, :], 1.0), inc=False)
        gop(lambda e: e.memset(idb[:, :], 1.0))
        W("gpsimd", gS, gS.n)
        gop(lambda e: e.affine_select(idb[:, :], idb[:, :],
            pattern=[[-1, 128]], compare_op=OP.is_equal, fill=0.0,
            base=0, channel_multiplier=1))
        W("gpsimd", gS, gS.n)
        gop(lambda e: e.memset(m_le[:, :], 1.0))
        W("gpsimd", gS, gS.n)
        gop(lambda e: e.affine_select(m_le[:, :], m_le[:, :],
            pattern=[[-1, 128]], compare_op=OP.is_ge, fill=0.0,
            base=0, channel_multiplier=1))
        W("gpsimd", gS, gS.n)
        gop(lambda e: e.memset(m_ge[:, :], 1.0))
        W("gpsimd", gS, gS.n)
        g_setup = gop(lambda e: e.affine_select(m_ge[:, :], m_ge[:, :],
            pattern=[[1, 128]], compare_op=OP.is_ge, fill=0.0,
            base=0, channel_multiplier=-1))
        for eng in ("vector", "scalar", "tensor"):
            W(eng, gS, g_setup)

        dma(cTs[:, :], d_cT, ctr=d0)
        dma(gscs[:, :], d_gsc, ctr=d0)
        dma(uscs[:, :], d_usc, ctr=d0)
        dma(dscs[:, :], d_dsc, ctr=d0)
        d_small = dma(halocol[:, :], d_halo, ctr=d0)
        for eng in ("vector", "scalar", "tensor"):
            W(eng, d0, d_small)

        gate(1)
        a_siluc = aop(lambda e: e.activation(silucT[:, :], cTs[:, :], AF.Silu))
        W("tensor", aS, a_siluc)
        W("vector", aS, a_siluc)

        s0 = contextlib.ExitStack()
        wmb = [s0.enter_context(nc.sbuf_tensor(f"wmb{i}", [128, 16 * 512], F32))
               for i in range(2)]
        wmbB = [s0.enter_context(
            nc.sbuf_tensor(f"wmbB{i}", [128, 16 * 512], BF16)) for i in range(2)]
        silucB = s0.enter_context(nc.sbuf_tensor("silucB", [128, DC], BF16))
        mrl = s0.enter_context(nc.sbuf_tensor("mrl", [1, 3 * 512], F32))
        mclh = s0.enter_context(nc.sbuf_tensor("mclh", [128, 96], BF16))
        mcll = s0.enter_context(nc.sbuf_tensor("mcll", [128, 96], BF16))
        g96 = s0.enter_context(nc.sbuf_tensor("g96", [96, 128], F32))
        ghi = s0.enter_context(nc.sbuf_tensor("ghi", [96, 128], BF16))
        glo = s0.enter_context(nc.sbuf_tensor("glo", [96, 128], BF16))
        row_marks = [0, 0]
        wm_consumed = [0, 0]
        mod_store_done = [0]
        vop(lambda e: e.tensor_copy(silucB[:, :], silucT[:, :]))
        W("vector", vS, vS.n)
        for nf in range(24):
            par = nf % 2
            buf = wmb[par]
            W("sync", tS, wm_consumed[par])
            d = None
            for dq in range(4):
                src = d_wmods[dq * 512:(dq + 1) * 512,
                              nf * 512:(nf + 1) * 512].rearrange(
                                  "(a p) f -> p a f", p=128)
                d = dma(buf[:, dq * 2048:(dq + 1) * 2048], src, ctr=wmS[par])
            ga_chunk = nf in (8, 9, 10, 11, 20, 21, 22, 23)
            if ga_chunk:
                W("tensor", wmS[par], d)
                lhs_c, rhs_b = silucT, buf
            else:
                W("vector", wmS[par], d)
                W("vector", tS, wm_consumed[par])
                vop(lambda e, a=wmbB[par][:, :], b=buf[:, :]:
                    e.tensor_copy(a, b))
                cb_ = vfence()
                W("tensor", vS, cb_)
                lhs_c, rhs_b = silucB, wmbB[par]
            ps = PB[par]
            W("tensor", vS, row_marks[par])
            t_row = None
            for dc in range(DC):
                t_row = mm(ps[0:1, :], lhs_c[:, dc:dc + 1],
                           rhs_b[:, dc * 512:(dc + 1) * 512],
                           start=(dc == 0), stop=(dc == DC - 1),
                           inc=(dc == DC - 1))
            wm_consumed[par] = t_row
            W("vector", tS, t_row)
            W("vector", msS, mod_store_done[0])
            mrf = mrl[0:1, 0:512]
            v_row = vop(lambda e, a=mrf, b=ps[0:1, :]: e.tensor_copy(a, b))
            row_marks[par] = v_row
            W("sync", vS, v_row)
            mod_store_done[0] = dma(d_mrows[nf * 4:(nf + 1) * 4, :], mrf,
                                    ctr=msS)
        W("sync", msS, msS.n)
        d_g = dma(g96[:, :], d_mrows, ctr=mcS)
        W("vector", mcS, d_g)
        vop(lambda e: e.tensor_copy(ghi[:, :], g96[:, :]))
        W("vector", vS, vS.n)
        v_glo = vop(lambda e: e.tensor_sub(glo[:, :], g96[:, :], ghi[:, :]))
        W("sync", vS, v_glo)
        dma(d_mhi, ghi[:, :], ctr=mcS)
        dma(d_mlo, glo[:, :], ctr=mcS)
        W("sync", mcS, mcS.n)
        emit("sync", lambda e: e.dma_start_transpose(mclh[:, :], d_mhi)
             .then_inc(mcS.h, 16))
        mcS.bump(16)
        emit("sync", lambda e: e.dma_start_transpose(mcll[:, :], d_mlo)
             .then_inc(mcS.h, 16))
        d_tr = mcS.bump(16)
        W("vector", mcS, d_tr)
        vop(lambda e: e.tensor_tensor(modcol[:, :], mclh[:, :], mcll[:, :],
                                      OP.add))
        s0.close()
        W("vector", vS, vS.n)
        vop(lambda e: e.tensor_scalar_add(modp[:, 16:32], modcol[:, 16:32], 1.0))
        vop(lambda e: e.tensor_scalar_add(modp[:, 64:80], modcol[:, 64:80], 1.0))
        W("vector", vS, vS.n)
        for gi, s_ in enumerate((32, 80)):
            tc0 = gi * 16
            vop(lambda e, s=s_, t=tc0: e.tensor_single_scalar(
                stg[:, 64 + t:64 + t + 16], modcol[:, s:s + 16], 0.0,
                OP.is_ge))
            W("vector", vS, vS.n)
            vop(lambda e, t=tc0: e.tensor_scalar_mul(
                stg[:, 96 + t:96 + t + 16], stg[:, 64 + t:64 + t + 16],
                2.0))
            W("vector", vS, vS.n)
            vop(lambda e, t=tc0: e.tensor_scalar_add(
                stg[:, 128 + t:128 + t + 16], stg[:, 96 + t:96 + t + 16],
                -1.0))
            W("vector", vS, vS.n)
            W("scalar", vS, vS.n)
            a_abs = aop(lambda e, s=s_, t=tc0: e.activation(
                stg[:, t:t + 16], modcol[:, s:s + 16], AF.Abs))
            W("vector", aS, a_abs)
            vop(lambda e, t=tc0: e.tensor_scalar_max(
                stg[:, 32 + t:32 + t + 16], stg[:, t:t + 16], 0.25))
            W("vector", vS, vS.n)
            vop(lambda e, s=s_, t=tc0: e.tensor_mul(
                modp[:, s:s + 16], stg[:, 128 + t:128 + t + 16],
                stg[:, 32 + t:32 + t + 16]))
            W("vector", vS, vS.n)
        v_mod = vop(lambda e: e.tensor_mul(dscgb[:, :], dscs[:, :],
                                           modp[:, 80:96]))
        W("scalar", vS, v_mod)
        W("tensor", vS, v_mod)
        stg_marks[0] = v_mod
        stg_marks[1] = v_mod

        h1 = en(nc.sbuf_tensor("h1", [128, DC * TKV], BF16))
        if upto == 2:
            dbg_h1[0] = h1

        def h1_at(dc, t0, n):
            return h1[:, dc * TKV + t0: dc * TKV + t0 + n]

        def ln_phase(src_at, dst_at, nstrips, scale_c, bias_c, lntmp):
            lnb = lntmp.ap().bitcast(BF16)
            stat_slot = [0, 0]
            act_slot = [0, 0]
            bcast_war = [0, 0]
            for s in range(nstrips):
                par = s % 2
                psum_s = PB[4 + 2 * par]
                psum_q = PB[5 + 2 * par]
                pbm = PB[0 + 2 * par]
                pbr = PB[1 + 2 * par]
                for dc in range(DC):
                    slot = dc % 2
                    xb = lnb[:, 4096 + slot * 1024: 4096 + slot * 1024 + 512]
                    xq = lnb[:, 4608 + slot * 1024: 4608 + slot * 1024 + 512]
                    W("vector", tS, stat_slot[slot])
                    vop(lambda e, a=xb, b=src_at(dc, s):
                        e.tensor_copy(a, b))
                    v_c = vfence()
                    W("scalar", vS, v_c)
                    aop(lambda e, a=xq, b=xb: e.activation(a, b, AF.Square))
                    a_q = afence()
                    W("tensor", vS, v_c)
                    mm(psum_s[0:1, :], onescb[:, :], xb,
                       start=(dc == 0), stop=(dc == DC - 1))
                    W("tensor", aS, a_q)
                    stat_slot[slot] = mm(psum_q[0:1, :], onescb[:, :], xq,
                                         start=(dc == 0),
                                         stop=(dc == DC - 1), inc=True)
                W("vector", tS, stat_slot[(DC - 1) % 2])
                mrow = rows[0:1, 0:512]
                qrow = rows[0:1, 512:1024]
                vrow = rows[0:1, 1024:1536]
                rrow = rows[0:1, 1536:2048]
                vop(lambda e, a=mrow, b=psum_s[0:1, :]:
                    e.tensor_scalar_mul(a, b, 1.0 / D))
                vop(lambda e, a=qrow, b=psum_q[0:1, :]:
                    e.tensor_scalar_mul(a, b, 1.0 / D))
                W("vector", vS, vS.n)
                vop(lambda e, a=rrow, b=mrow: e.tensor_mul(a, b, b))
                W("vector", vS, vS.n)
                v_var = vop(lambda e, a=vrow, b=qrow, c=rrow:
                            e.tensor_sub(a, b, c))
                W("scalar", vS, v_var)
                aop(lambda e, a=rrow, b=vrow: e.activation(a, b, AF.Ln,
                                                           bias=EPS))
                W("scalar", aS, aS.n)
                a_rstd = aop(lambda e, a=qrow, b=rrow: e.activation(
                    a, b, AF.Exp, scale=-0.5))
                W("tensor", aS, a_rstd)
                W("tensor", vS, bcast_war[par])
                mm(pbm[:, :], ones1f[0:1, :], mrow, start=True, stop=True)
                t_bc = mm(pbr[:, :], ones1f[0:1, :], qrow, start=True,
                          stop=True, inc=True)
                W("vector", tS, t_bc)
                for dc in range(DC):
                    slot = dc % 2
                    t1 = lntmp[:, slot * 1024: slot * 1024 + 512]
                    t2 = lntmp[:, slot * 1024 + 512: slot * 1024 + 1024]
                    vop(lambda e, a=t1, b=src_at(dc, s), c=pbm[:, :]:
                        e.tensor_sub(a, b, c))
                    W("vector", vS, vS.n)
                    vop(lambda e, a=t2, b=t1, c=pbr[:, :]:
                        e.tensor_mul(a, b, c))
                    W("vector", vS, vS.n)
                    act_slot[slot] = vop(
                        lambda e, a=dst_at(dc, s), b=t2, sc=scale_c(dc),
                        bi=bias_c(dc): e.tensor_scalar(a, b, sc, bi,
                                                       OP.mult, OP.add))
                bcast_war[par] = act_slot[(DC - 1) % 2]

        gate(2)
        s1s = contextlib.ExitStack()
        Xb = s1s.enter_context(nc.sbuf_tensor("Xb", [128, DC * TOK], F32))
        Xh = s1s.enter_context(nc.sbuf_tensor("Xh", [128, DC * HALO], F32))
        lntmp1 = s1s.enter_context(nc.sbuf_tensor("lntmp1", [128, 3072], F32))
        W("sync", vS, vS.n)
        d1 = dma(Xh[:, :], d_xh.rearrange("(a p) t -> p a t", p=128), ctr=dXh)
        d2 = dma(Xb[:, :], d_xo.rearrange("(a p) t -> p a t", p=128), ctr=dXo)
        W("vector", dXh, d1)
        ln_phase(lambda dc, s: Xh[:, dc * HALO: dc * HALO + 512],
                 lambda dc, s: h1_at(dc, 0, 512), 1, ca1_c, sa_c, lntmp1)
        W("vector", dXo, d2)
        ln_phase(lambda dc, s: Xb[:, dc * TOK + s * 512:
                                  dc * TOK + s * 512 + 512],
                 lambda dc, s: h1_at(dc, HALO + s * 512, 512),
                 2, ca1_c, sa_c, lntmp1)
        ln1_done = vS.n
        s1s.close()

        gate(3)
        sA = contextlib.ExitStack()
        obuf = sA.enter_context(nc.sbuf_tensor("obuf", [128, NH * TOK], BF16))
        if upto == 3:
            dbg_h1[0] = obuf
        s23 = contextlib.ExitStack()
        wgq = s23.enter_context(nc.sbuf_tensor("wgq", [128, DC * 512], BF16))
        wgk = s23.enter_context(nc.sbuf_tensor("wgk", [128, DC * 512], BF16))
        wgv = s23.enter_context(nc.sbuf_tensor("wgv", [128, DC * 512], BF16))
        qg = s23.enter_context(nc.sbuf_tensor("qg", [128, 4 * TOK], BF16))
        kg = s23.enter_context(nc.sbuf_tensor("kg", [128, 4 * TKV], BF16))
        vg = s23.enter_context(nc.sbuf_tensor("vg", [128, 12 * 512], BF16))
        pT = s23.enter_context(nc.sbuf_tensor("pT", [128, 8 * 512], BF16))
        nmrow = s23.enter_context(nc.sbuf_tensor("nmrow", [1, 512], BF16))
        nmcols = s23.enter_context(nc.sbuf_tensor("nmcols", [128, 6], F32))
        rlBs = s23.enter_context(nc.sbuf_tensor("rlBs", [128, 512], F32))

        W("tensor", vS, ln1_done)

        def wload(dst, col0, g, consumed_mark):
            W("sync", tS, consumed_mark)
            cm = None
            for dq in range(4):
                src = d_wqkvT[dq * 512:(dq + 1) * 512,
                              col0 + g * 512: col0 + (g + 1) * 512] \
                    .rearrange("(a p) f -> p a f", p=128)
                cm = stage_load_convert(dst[:, dq * 2048:(dq + 1) * 2048], src)
            return cm

        kcw = [128, 256, 384, 512, 512, 384, 256, 128]
        kcq0 = [0, 0, 0, 0, 0, 128, 256, 384]
        kcpk = [(0, 0), (1, 0), (2, 0), (3, 0), (4, 0), (5, 0),
                (1, 256), (0, 128)]

        kgrp = int(os.environ.get("KGRP", "4"))
        g_weights_consumed = [0]
        evac_q = [0, 0]
        evac_k = [0, 0]
        evac_v = [0, 0]
        attn_last_vo = [0]
        attn_last_exp = [0]

        for g in range(kgrp):
            wload(wgq, 0, g, g_weights_consumed[0])
            wload(wgk, D, g, g_weights_consumed[0])
            wload(wgv, 2 * D, g, g_weights_consumed[0])
            cv = vfence()
            W("tensor", vS, cv)

            if ksub == "onemm" and g == 0:
                mm(PB[0][:, :], wgq[:, 0:128], h1_at(0, HALO, 512),
                   start=True, stop=True, inc=True)
                t1mm = tS.n
                W("vector", tS, t1mm)
                vop(lambda e: e.tensor_copy(qg[:, 0:512], PB[0][:, :]))
            if ksub in ("dcloop", "dcevac", "dcevacf", "dcevaca") and g == 0:
                for dc in range(DC):
                    wsl = wgq[:, dc * 512: dc * 512 + 128]
                    mm(PB[0][:, :], wsl, h1_at(dc, HALO, 512),
                       start=(dc == 0), stop=(dc == DC - 1))
                    mm(PB[1][:, :], wsl, h1_at(dc, HALO + 512, 512),
                       start=(dc == 0), stop=(dc == DC - 1),
                       inc=(dc == DC - 1))
                t1mm = tS.n
                W("vector", tS, t1mm)
                if ksub == "dcevac":
                    vop(lambda e: e.tensor_copy(qg[:, 0:512], PB[0][:, :]))
                    vop(lambda e: e.tensor_copy(qg[:, 512:1024], PB[1][:, :]))
                if ksub == "dcevacf":
                    vop(lambda e: e.tensor_copy(stg[:, 0:512], PB[0][:, :]))
                    vop(lambda e: e.tensor_copy(stg[:, 512:1024], PB[1][:, :]))
                if ksub == "dcevaca":
                    aop(lambda e: e.copy(qg[:, 0:512], PB[0][:, :]))
                    aop(lambda e: e.copy(qg[:, 512:1024], PB[1][:, :]))
            nfs_q = 0 if ksub in ("wload", "onemm", "dcloop", "dcevac", "dcevacf", "dcevaca") else int(os.environ.get("KFS", "4"))
            for fs in range(nfs_q):
                par = fs % 2
                p0, p1 = PB[par * 2], PB[par * 2 + 1]
                W("tensor", vS, evac_q[par])
                tq = None
                for dc in range(DC):
                    wsl = wgq[:, dc * 512 + fs * 128: dc * 512 + fs * 128 + 128]
                    mm(p0[:, :], wsl, h1_at(dc, HALO, 512),
                       start=(dc == 0), stop=(dc == DC - 1))
                    tq = mm(p1[:, :], wsl, h1_at(dc, HALO + 512, 512),
                            start=(dc == 0), stop=(dc == DC - 1),
                            inc=(dc == DC - 1))
                W("vector", tS, tq)
                vop(lambda e, a=qg[:, fs * TOK: fs * TOK + 512], b=p0[:, :]:
                    e.tensor_copy(a, b))
                evac_q[par] = vop(
                    lambda e, a=qg[:, fs * TOK + 512:(fs + 1) * TOK],
                    b=p1[:, :]: e.tensor_copy(a, b))
            for fs in (range(4) if ksub not in ("q", "wload", "onemm", "dcloop", "dcevac", "dcevacf", "dcevaca") else []):
                par = fs % 2
                pj = [PB[4 + par * 2], PB[5 + par * 2], PB[par * 2]]
                W("tensor", vS, evac_k[par])
                W("tensor", vS, evac_q[par])
                tk = None
                for dc in range(DC):
                    wsl = wgk[:, dc * 512 + fs * 128: dc * 512 + fs * 128 + 128]
                    for j in range(3):
                        tk = mm(pj[j][:, :], wsl, h1_at(dc, j * 512, 512),
                                start=(dc == 0), stop=(dc == DC - 1),
                                inc=(dc == DC - 1 and j == 2))
                W("vector", tS, tk)
                vk = None
                for j in range(3):
                    vk = vop(lambda e, a=kg[:, fs * TKV + j * 512:
                                            fs * TKV + (j + 1) * 512],
                             b=pj[j][:, :]: e.tensor_copy(a, b))
                evac_k[par] = vk
                evac_q[par] = vk
            W("tensor", vS, evac_k[1])
            for tc in (range(12) if ksub not in ("q", "qk", "wload", "onemm", "dcloop", "dcevac", "dcevacf", "dcevaca") else []):
                par = tc % 2
                pv = PB[6 + par]
                W("tensor", vS, evac_v[par])
                tv = None
                for dc in range(DC):
                    tv = mm(pv[:, :], h1_at(dc, tc * 128, 128),
                            wgv[:, dc * 512: dc * 512 + 512],
                            start=(dc == 0), stop=(dc == DC - 1),
                            inc=(dc == DC - 1))
                W("vector", tS, tv)
                evac_v[par] = vop(
                    lambda e, a=vg[:, tc * 512:(tc + 1) * 512],
                    b=pv[:, :]: e.tensor_copy(a, b))
            g_weights_consumed[0] = tS.n
            W("tensor", vS, max(evac_v))
            W("tensor", vS, max(evac_k))

            n_att = [99]
            if ksub in ("prod", "q", "qk", "qkv", "wload", "onemm", "dcloop", "dcevac", "dcevacf", "dcevaca"):
                n_att = [0]
            elif ksub.startswith("att"):
                n_att = [int(ksub[3:])]
            att_count = [0]
            for hh in range(4):
                h = g * 4 + hh
                for n in range(2):
                    if att_count[0] >= n_att[0]:
                        continue
                    att_count[0] += 1
                    qwin = qg[:, hh * TOK + n * 512: hh * TOK + n * 512 + 512]
                    kwin0 = hh * TKV + n * 512

                    def k_at(ka, w, _k0=kwin0):
                        return kg[:, _k0 + ka: _k0 + ka + w]

                    W("tensor", vS, attn_last_vo[0])
                    W("tensor", aS, attn_last_exp[0])
                    red_marks = []
                    for qc in range(4):
                        pa, pb_ = PB[qc * 2], PB[qc * 2 + 1]
                        mm(pa[:, :], qwin[:, qc * 128:(qc + 1) * 128],
                           k_at(0, 512), start=True, stop=True)
                        tqc = mm(pb_[:, :], qwin[:, qc * 128:(qc + 1) * 128],
                                 k_at(512, 512), start=True, stop=True,
                                 inc=True)
                        W("vector", tS, tqc)
                        nmc = nmcols[:, qc:qc + 1]
                        vop(lambda e, a=nmcols[:, 4:5], b=pa[:, :]:
                            e.reduce_max(a, b, AX, negate=True))
                        vop(lambda e, a=nmcols[:, 5:6], b=pb_[:, :]:
                            e.reduce_max(a, b, AX, negate=True))
                        W("vector", vS, vS.n)
                        red_marks.append(vop(
                            lambda e, a=nmc, b=nmcols[:, 4:5],
                            c=nmcols[:, 5:6]:
                            e.tensor_tensor(a, b, c, OP.min)))
                        W("vector", vS, vS.n)
                    dsc_marks = []
                    for qc in range(4):
                        dsc_marks.append(vop(
                            lambda e, o=diagsc[:, :], i=idb[:, :],
                            sc=nmcols[:, qc:qc + 1]:
                            e.tensor_scalar_mul(o, i, sc)))
                        W("tensor", vS, dsc_marks[-1])
                        t_nm = mm(PB[6][0:1, qc * 128:(qc + 1) * 128],
                                  onescb[:, :], diagsc[:, :],
                                  start=(qc == 0), stop=(qc == 3),
                                  inc=True)
                        W("vector", tS, t_nm)
                    v_nmr = vop(lambda e, a=nmrow[0:1, :],
                                b=PB[6][0:1, 0:512]: e.tensor_copy(a, b))
                    W("tensor", vS, v_nmr)
                    first_in_bank = set()
                    for kc in range(8):
                        bnk, c0 = kcpk[kc]
                        w_ = kcw[kc]
                        mm(PB[bnk][:, c0:c0 + w_], k_at(kc * 128, 128),
                           qwin[:, kcq0[kc]:kcq0[kc] + w_],
                           start=(bnk not in first_in_bank), stop=False)
                        first_in_bank.add(bnk)
                    t_s2 = None
                    for kc in range(8):
                        bnk, c0 = kcpk[kc]
                        w_ = kcw[kc]
                        t_s2 = mm(PB[bnk][:, c0:c0 + w_], ones1b[0:1, :],
                                  nmrow[0:1, kcq0[kc]:kcq0[kc] + w_],
                                  start=False, stop=(kc >= 2), inc=(kc == 7))
                    W("scalar", tS, t_s2)
                    a_exp = None
                    for kc in range(8):
                        bnk, c0 = kcpk[kc]
                        w_ = kcw[kc]
                        a_exp = aop(lambda e,
                                    o=pT[:, kc * 512: kc * 512 + w_],
                                    i=PB[bnk][:, c0:c0 + w_]:
                                    e.activation(o, i, AF.Exp, scale=SCALE))
                    attn_last_exp[0] = a_exp
                    W("vector", aS, a_exp)
                    v_m = None
                    for kc in range(8):
                        if kc < 4:
                            eb = kc * 128
                            msk = m_le
                        else:
                            eb = 0
                            msk = m_ge
                        v_m = vop(lambda e, a=pT[:, kc * 512 + eb:
                                                 kc * 512 + eb + 128],
                                  b=msk[:, :]: e.tensor_mul(a, a, b))
                        if n == 0 and kc < 4:
                            W("vector", vS, vS.n)
                            v_m = vop(lambda e, a=pT[:, kc * 512:
                                                     kc * 512 + kcw[kc]]:
                                      e.tensor_scalar_mul(a, a,
                                                          halocol[:, :]))
                    W("tensor", vS, v_m)
                    kc_order = [3, 4, 2, 1, 0, 5, 6, 7]
                    t_l = None
                    for ki, kc in enumerate(kc_order):
                        t_l = mm(PB[7][0:1, kcq0[kc]:kcq0[kc] + kcw[kc]],
                                 onescb[:, :],
                                 pT[:, kc * 512: kc * 512 + kcw[kc]],
                                 start=(ki == 0), stop=(ki == 7),
                                 inc=(ki == 7))
                    t_pv = None
                    for ki, kc in enumerate(kc_order):
                        tcg = n * 4 + kc
                        vsl = vg[:, tcg * 512 + hh * 128:
                                 tcg * 512 + hh * 128 + 128]
                        t_pv = mm(PB[6][:, kcq0[kc]:kcq0[kc] + kcw[kc]], vsl,
                                  pT[:, kc * 512: kc * 512 + kcw[kc]],
                                  start=(ki == 0), stop=(ki == 7),
                                  inc=(ki == 7))
                    W("scalar", tS, t_l)
                    lr = rows[0:1, 0:512]
                    lr2 = rows[0:1, 512:1024]
                    aop(lambda e, a=lr, b=PB[7][0:1, 0:512]:
                        e.activation(a, b, AF.Ln))
                    W("scalar", aS, aS.n)
                    a_rl = aop(lambda e, a=lr2, b=lr: e.activation(
                        a, b, AF.Exp, scale=-1.0))
                    W("tensor", aS, a_rl)
                    t_rlb = mm(PB[7][:, :], ones1f[0:1, :], lr2,
                               start=True, stop=True, inc=True)
                    W("vector", tS, t_rlb)
                    W("vector", tS, t_pv)
                    vop(lambda e, a=rlBs[:, :], b=PB[7][:, :]:
                        e.tensor_copy(a, b))
                    W("vector", vS, vS.n)
                    attn_last_vo[0] = vop(
                        lambda e, a=obuf[:, h * TOK + n * 512:
                                         h * TOK + n * 512 + 512],
                        b=PB[6][:, :], c=rlBs[:, :]: e.tensor_mul(a, b, c))
        W("tensor", vS, attn_last_vo[0])
        s23.close()

        gate(4)
        s45 = contextlib.ExitStack()
        X2 = s45.enter_context(nc.sbuf_tensor("X2", [128, DC * TOK], F32))
        if upto == 4:
            dbg_x2[0] = X2
        s4 = contextlib.ExitStack()
        wo = [s4.enter_context(nc.sbuf_tensor(f"wo{i}", [128, DC * 256], BF16))
              for i in range(2)]
        xt = s4.enter_context(nc.sbuf_tensor("xt", [128, 4 * 512], F32))
        xt_marks = [0, 0, 0, 0]
        n_xt = [0]
        x2_marks = [0, 0]
        wo_consumed = [0, 0]
        wo_ready = [0, 0]

        def load_wo(ocb):
            par = ocb % 2
            W("sync", tS, wo_consumed[par])
            for dq in range(4):
                src = d_woutT[dq * 512:(dq + 1) * 512,
                              ocb * 256:(ocb + 1) * 256] \
                    .rearrange("(a p) f -> p a f", p=128)
                stage_load_convert(wo[par][:, dq * 1024:(dq + 1) * 1024], src)
            wo_ready[par] = vfence()

        load_wo(0)
        for ocb in range(8):
            par = ocb % 2
            if ocb < 7:
                load_wo(ocb + 1)
            W("tensor", vS, wo_ready[par])
            for ec in range(2):
                e_g = ocb * 2 + ec
                ppar = ec % 2
                p0, p1 = PB[ppar * 2], PB[ppar * 2 + 1]
                sl0, sl1 = n_xt[0] % 4, (n_xt[0] + 1) % 4
                n_xt[0] += 2
                x0 = xt[:, sl0 * 512:(sl0 + 1) * 512]
                x1 = xt[:, sl1 * 512:(sl1 + 1) * 512]
                W("sync", vS, xt_marks[sl0])
                dx0 = dma(x0, d_xo[e_g * 128:(e_g + 1) * 128, 0:512],
                          ctr=xtS[sl0])
                W("sync", vS, xt_marks[sl1])
                dx1 = dma(x1, d_xo[e_g * 128:(e_g + 1) * 128, 512:1024],
                          ctr=xtS[sl1])
                W("tensor", vS, x2_marks[ppar])
                t_ec = None
                for kf in range(DC):
                    wsl = wo[par][:, kf * 256 + ec * 128:
                                  kf * 256 + ec * 128 + 128]
                    mm(p0[:, :], wsl, obuf[:, kf * TOK: kf * TOK + 512],
                       start=(kf == 0), stop=(kf == DC - 1))
                    t_ec = mm(p1[:, :], wsl,
                              obuf[:, kf * TOK + 512:(kf + 1) * TOK],
                              start=(kf == 0), stop=(kf == DC - 1),
                              inc=(kf == DC - 1))
                W("vector", tS, t_ec)
                W("vector", xtS[sl0], dx0)
                W("vector", xtS[sl1], dx1)
                if ksub == "wraw":
                    xt_marks[sl0] = vop(
                        lambda e, o=X2[:, e_g * TOK: e_g * TOK + 512],
                        p=p0[:, :]: e.tensor_copy(o, p))
                    v_x2 = vop(
                        lambda e, o=X2[:, e_g * TOK + 512:(e_g + 1) * TOK],
                        p=p1[:, :]: e.tensor_copy(o, p))
                else:
                    xt_marks[sl0] = vop(
                        lambda e, o=X2[:, e_g * TOK: e_g * TOK + 512],
                        p=p0[:, :], sc=ga_c(e_g), x=x0:
                        e.scalar_tensor_tensor(o, p, sc, x, OP.mult, OP.add))
                    v_x2 = vop(
                        lambda e, o=X2[:, e_g * TOK + 512:(e_g + 1) * TOK],
                        p=p1[:, :], sc=ga_c(e_g), x=x1:
                        e.scalar_tensor_tensor(o, p, sc, x, OP.mult, OP.add))
                xt_marks[sl1] = v_x2
                x2_marks[ppar] = v_x2
            wo_consumed[par] = tS.n
        s4.close()

        gate(5)
        s5 = contextlib.ExitStack()
        lntmp2 = s5.enter_context(nc.sbuf_tensor("lntmp2", [128, 3072], F32))
        ln_phase(lambda dc, s: X2[:, dc * TOK + s * 512:
                                  dc * TOK + s * 512 + 512],
                 lambda dc, s: h1_at(dc, HALO + s * 512, 512),
                 2, cb1_c, sb_c, lntmp2)
        s5.close()
        if upto == 5:
            dbg_h1[0] = h1
        ln2_done = vS.n
        W("sync", vS, vS.n)
        d_spill = dma(d_x2.rearrange("(a p) t -> p a t", p=128), X2[:, :],
                      ctr=spS)
        W("vector", spS, d_spill)
        s45.close()
        sA.close()

        h2_at = lambda dc, n: h1_at(dc, HALO + n * 512, 512)
        W("tensor", vS, ln2_done)

        gate(6)
        fuse = en(nc.sbuf_tensor("fuse", [128, FC * TOK], BF16))
        s6 = contextlib.ExitStack()
        gwb = [s6.enter_context(nc.sbuf_tensor(f"gwb{i}", [128, DC * 512], FP8))
               for i in range(2)]
        uwb = [s6.enter_context(nc.sbuf_tensor(f"uwb{i}", [128, DC * 512], FP8))
               for i in range(2)]
        gt = s6.enter_context(nc.sbuf_tensor("gt", [128, 2 * 512], BF16))
        ut = s6.enter_context(nc.sbuf_tensor("ut", [128, 2 * 512], BF16))
        vf_marks = [0, 0]
        gu_consumed = [0, 0]
        gu_ready = [0, 0]

        def load_gu(fb):
            par = fb % 2
            W("sync", tS, gu_consumed[par])
            for dq in range(4):
                src = d_gw[dq * 512:(dq + 1) * 512,
                           fb * 512:(fb + 1) * 512] \
                    .rearrange("(a p) f -> p a f", p=128)
                stage_load_convert(gwb[par][:, dq * 2048:(dq + 1) * 2048], src)
            for dq in range(4):
                src = d_uw[dq * 512:(dq + 1) * 512,
                           fb * 512:(fb + 1) * 512] \
                    .rearrange("(a p) f -> p a f", p=128)
                stage_load_convert(uwb[par][:, dq * 2048:(dq + 1) * 2048], src)
            gu_ready[par] = vfence()

        load_gu(0)
        for fb in range(12):
            par = fb % 2
            if fb < 11:
                load_gu(fb + 1)
            W("tensor", vS, gu_ready[par])
            for f2 in range(4):
                fc = fb * 4 + f2
                ppar = f2 % 2
                pg = [PB[ppar * 4 + 0], PB[ppar * 4 + 1]]
                pu = [PB[ppar * 4 + 2], PB[ppar * 4 + 3]]
                W("tensor", vS, vf_marks[ppar])
                t_fc = None
                for dc in range(DC):
                    gsl = gwb[par][:, dc * 512 + f2 * 128:
                                   dc * 512 + f2 * 128 + 128]
                    usl = uwb[par][:, dc * 512 + f2 * 128:
                                   dc * 512 + f2 * 128 + 128]
                    mm(pg[0][:, :], gsl, h2_at(dc, 0),
                       start=(dc == 0), stop=(dc == DC - 1))
                    mm(pg[1][:, :], gsl, h2_at(dc, 1),
                       start=(dc == 0), stop=(dc == DC - 1))
                    mm(pu[0][:, :], usl, h2_at(dc, 0),
                       start=(dc == 0), stop=(dc == DC - 1))
                    t_fc = mm(pu[1][:, :], usl, h2_at(dc, 1),
                              start=(dc == 0), stop=(dc == DC - 1),
                              inc=(dc == DC - 1))
                W("scalar", tS, t_fc)
                W("vector", tS, t_fc)
                W("scalar", vS, vf_marks[1 - ppar])
                v_f = None
                for n in range(2):
                    a_g = aop(lambda e, o=gt[:, n * 512:(n + 1) * 512],
                              i=pg[n][:, :], sc=gscs[:, fc:fc + 1]:
                              e.activation(o, i, AF.Silu, scale=sc))
                    vop(lambda e, o=ut[:, n * 512:(n + 1) * 512],
                        i=pu[n][:, :], sc=uscs[:, fc:fc + 1]:
                        e.tensor_scalar_mul(o, i, sc))
                    W("vector", vS, vS.n)
                    W("vector", aS, a_g)
                    v_f = vop(lambda e, o=fuse[:, fc * TOK + n * 512:
                                               fc * TOK + n * 512 + 512],
                              a=gt[:, n * 512:(n + 1) * 512],
                              b=ut[:, n * 512:(n + 1) * 512]:
                              e.tensor_mul(o, a, b))
                vf_marks[ppar] = v_f
            gu_consumed[par] = tS.n
        s6.close()

        gate(7)
        s7 = contextlib.ExitStack()
        dwb = [s7.enter_context(nc.sbuf_tensor(f"dwb{i}", [128, FC * 256], FP8))
               for i in range(2)]
        x2t = s7.enter_context(nc.sbuf_tensor("x2t", [128, 2 * 512], F32))
        ob = s7.enter_context(nc.sbuf_tensor("ob", [128, 2 * 512], F32))
        x2t_marks = [0, 0]
        dn_marks = [0, 0]
        dw_consumed = [0, 0]
        dw_ready = [0, 0]
        W("sync", spS, d_spill)

        def load_dw(ecb):
            par = ecb % 2
            W("sync", tS, dw_consumed[par])
            for fq in range(6):
                src = d_dw[fq * 1024:(fq + 1) * 1024,
                           ecb * 256:(ecb + 1) * 256] \
                    .rearrange("(a p) f -> p a f", p=128)
                stage_load_convert(dwb[par][:, fq * 2048:(fq + 1) * 2048], src)
            dw_ready[par] = vfence()

        load_dw(0)
        for ecb in range(8):
            par = ecb % 2
            if ecb < 7:
                load_dw(ecb + 1)
            W("tensor", vS, dw_ready[par])
            for e2 in range(2):
                ec = ecb * 2 + e2
                ppar = e2 % 2
                p0, p1 = PB[ppar * 2], PB[ppar * 2 + 1]
                W("tensor", vS, dn_marks[ppar])
                xs0, xs1 = 0, 1
                W("sync", vS, x2t_marks[xs0])
                dx0 = dma(x2t[:, 0:512], d_x2[ec * 128:(ec + 1) * 128, 0:512],
                          ctr=x2S[xs0])
                W("sync", vS, x2t_marks[xs1])
                dx1 = dma(x2t[:, 512:1024],
                          d_x2[ec * 128:(ec + 1) * 128, 512:1024],
                          ctr=x2S[xs1])
                t_dn = None
                for fc in range(FC):
                    wsl = dwb[par][:, fc * 256 + e2 * 128:
                                   fc * 256 + e2 * 128 + 128]
                    mm(p0[:, :], wsl, fuse[:, fc * TOK: fc * TOK + 512],
                       start=(fc == 0), stop=(fc == FC - 1))
                    t_dn = mm(p1[:, :], wsl,
                              fuse[:, fc * TOK + 512:(fc + 1) * TOK],
                              start=(fc == 0), stop=(fc == FC - 1),
                              inc=(fc == FC - 1))
                W("vector", tS, t_dn)
                W("vector", x2S[xs0], dx0)
                W("vector", x2S[xs1], dx1)
                v_last = None
                for n in range(2):
                    osl = n % 2
                    W("vector", obS[osl], obS[osl].n)
                    v_last = vop(lambda e, o=ob[:, osl * 512:(osl + 1) * 512],
                                 p=(p0 if n == 0 else p1)[:, :],
                                 sc=dscgb[:, ec:ec + 1],
                                 x=x2t[:, n * 512:(n + 1) * 512]:
                                 e.scalar_tensor_tensor(o, p, sc, x,
                                                        OP.mult, OP.add))
                    if n == 1:
                        x2t_marks[0] = v_last
                        x2t_marks[1] = v_last
                    W("sync", vS, v_last)
                    dma(d_out[ec * 128:(ec + 1) * 128,
                              n * 512:(n + 1) * 512],
                        ob[:, osl * 512:(osl + 1) * 512], ctr=obS[osl])
                dn_marks[ppar] = v_last
            dw_consumed[par] = tS.n
        W("sync", obS[0], obS[0].n)
        W("sync", obS[1], obS[1].n)
        s7.close()

        blk = en(nc.Block())
        for eng in ENGS:
            def runner(e, _fns=tuple(ops[eng])):
                for f in _fns:
                    f(e)
            getattr(blk, eng)(runner)
    except _Done:
        pass

    return nc


_cache = {}


def kernel(L0_attn_x, L0_attn_w_qkv, L0_attn_w_out, L0_ffn_gate_w,
           L0_ffn_up_w, L0_ffn_down_w, L0_ffn_gate_scale, L0_ffn_up_scale,
           L0_ffn_down_scale, L0_c, L0_wmod):
    f32 = np.float32
    xT = np.ascontiguousarray(np.asarray(L0_attn_x, f32)[0].T)
    wqkvT = np.ascontiguousarray(np.asarray(L0_attn_w_qkv, f32).T)
    woutT = np.ascontiguousarray(np.asarray(L0_attn_w_out, f32).T)
    gw = np.ascontiguousarray(np.asarray(L0_ffn_gate_w, f32))
    uw = np.ascontiguousarray(np.asarray(L0_ffn_up_w, f32))
    dw = np.ascontiguousarray(np.asarray(L0_ffn_down_w, f32))
    gscT = np.ascontiguousarray(
        np.asarray(L0_ffn_gate_scale, f32).reshape(FC, 128).T)
    uscT = np.ascontiguousarray(
        np.asarray(L0_ffn_up_scale, f32).reshape(FC, 128).T)
    dscT = np.ascontiguousarray(
        np.asarray(L0_ffn_down_scale, f32).reshape(DC, 128).T)
    cT = np.ascontiguousarray(np.asarray(L0_c, f32).reshape(DC, 128).T)
    wmodc = np.ascontiguousarray(np.asarray(L0_wmod, f32))

    import os
    upto = int(os.environ.get("KUPTO", "99"))
    key = ("nc", upto)
    if key not in _cache:
        _cache[key] = build_nc(upto)
    nc = _cache[key]

    in_maps = []
    for c in range(NCORES):
        xo = np.ascontiguousarray(xT[:, c * TOK:(c + 1) * TOK])
        if c == 0:
            xhs = np.zeros((D, HALO), f32)
            hcol = np.zeros((128, 1), f32)
        else:
            xhs = np.ascontiguousarray(xT[:, c * TOK - HALO: c * TOK])
            hcol = np.ones((128, 1), f32)
        in_maps.append({
            "xo": xo, "xh": xhs, "wqkvT": wqkvT, "woutT": woutT,
            "gw": gw, "uw": uw, "dw": dw, "gscT": gscT, "uscT": uscT,
            "dscT": dscT, "cT": cT, "halo_on": hcol,
            "wmods": wmodc,
        })
    res = run_bass_kernel_spmd(nc, in_maps, core_ids=list(range(NCORES)))
    outT = np.concatenate([np.asarray(r["outT"]) for r in res.results],
                          axis=1)
    return np.ascontiguousarray(outT.T).reshape(1, 8192, D)
